# version 20
# baseline (speedup 1.0000x reference)
import sys
import numpy as np
import ml_dtypes
from contextlib import ExitStack

for _p in ("/opt/trn_rl_repo",):
    if _p not in sys.path:
        sys.path.insert(0, _p)

import concourse.bass as bass
import concourse.mybir as mybir
from concourse.bass_utils import run_bass_kernel_spmd

F32 = mybir.dt.float32
BF16 = mybir.dt.bfloat16
AF = mybir.ActivationFunctionType
ALU = mybir.AluOpType
AX = mybir.AxisListType

CH = 16
TH = 512
TO = 1024
TT = 1536
NE = 4
NH = 16
NKV = 4
NJ = 8
WK = 640
NSLOT = 4
NCORE = 8

_STAGE = "full"

SEMKEYS = (("X", "W", "T", "S", "V", "O", "G")
           + tuple(f"W{i}" for i in range(8))
           + tuple(f"C{i}" for i in range(8))
           + tuple(f"P{i}" for i in range(8))
           + ("Q0", "Q1"))


class Em:

    def __init__(self, dry, sems=None, events=None):
        self.dry = dry
        self.sems = sems
        self.cnt = {k: 0 for k in SEMKEYS}
        self.rec = {} if dry else (events or {})
        self.chain = None

    def op(self, fn, inckey=None, amt=1, event=None):
        ck = self.chain
        if ck is not None and inckey is None:
            inckey, amt = ck, 1
        if self.dry:
            if inckey:
                self.cnt[inckey] += amt
                if event is not None:
                    self.rec[event] = self.cnt[inckey]
            return None
        inst = fn()
        if ck is not None:
            prev = self.cnt[ck]
            if event is None and inckey == ck:
                pass
            if prev > 0:
                inst._wait_ge(self.sems[ck], prev)
        if inckey:
            inst.then_inc(self.sems[inckey], amt)
            self.cnt[inckey] += amt
            if event is not None:
                assert self.rec.get(event) == self.cnt[inckey], (
                    event, self.rec.get(event), self.cnt[inckey])
        return inst

    def ev(self, name):
        return self.rec[name]

    def wait_ev(self, eng, key, event):
        if self.dry:
            return
        v = self.rec[event]
        if v > 0:
            eng.wait_ge(self.sems[key], v)

    def wait_cur(self, eng, key):
        if self.dry:
            return
        v = self.cnt[key]
        if v > 0:
            eng.wait_ge(self.sems[key], v)

    def wait_val(self, eng, key, v):
        if self.dry:
            return
        if v > 0:
            eng.wait_ge(self.sems[key], v)


def build_nc(stage="full"):
    nc = bass.Bass("TRN2", target_bir_lowering=False)

    xT_d = nc.dram_tensor("xT", [2048, TT], F32, kind="ExternalInput")
    wq_d = nc.dram_tensor("wq", [NH, 128, CH * 128], F32, kind="ExternalInput")
    wk_d = nc.dram_tensor("wk", [128, CH, 512], F32, kind="ExternalInput")
    wv_d = nc.dram_tensor("wv", [128, CH, 512], F32, kind="ExternalInput")
    wo_d = nc.dram_tensor("wo", [CH, 128, CH * 128], F32, kind="ExternalInput")
    wg_d = nc.dram_tensor("wg", [NE, 32, 128, CH * 128], F32,
                          kind="ExternalInput")
    wd_d = nc.dram_tensor("wd", [NE, 16, 128, CH * 128], F32,
                          kind="ExternalInput")
    lam_d = nc.dram_tensor("lam", [128, NH], F32, kind="ExternalInput")
    cos_d = nc.dram_tensor("cosT", [128, TT], BF16, kind="ExternalInput")
    sin_d = nc.dram_tensor("sinT", [128, TT], BF16, kind="ExternalInput")
    maskE_d = nc.dram_tensor("maskE", [5, 128, 256], BF16,
                             kind="ExternalInput")
    maskM_d = nc.dram_tensor("maskM", [3, 128, 384], BF16,
                             kind="ExternalInput")
    id_d = nc.dram_tensor("identT", [128, 128], BF16, kind="ExternalInput")
    yT_d = nc.dram_tensor("yT", [2048, TO], F32, kind="ExternalOutput")
    wdc_d = nc.dram_tensor("wdc", [NE, 16, 128, 2048], BF16, kind="Internal")

    ctx = ExitStack()
    with ctx:
        sems = {k: ctx.enter_context(nc.semaphore("sem" + k)) for k in SEMKEYS}

        def sbuf(name, shape, dt, side="left"):
            return ctx.enter_context(nc.sbuf_tensor(name, shape, dt, side=side))

        x_own = sbuf("x_own", [128, CH, TO], F32)
        A_sb = sbuf("A_sb", [128, TT], BF16)
        B_sb = sbuf("B_sb", [128, TT], BF16)
        lam_sb = sbuf("lam_sb", [128, NH], F32)
        ident = sbuf("ident_sb", [128, 128], BF16)
        cosq = sbuf("cosq", [128, TO], BF16)
        sinq = sbuf("sinq", [128, TO], BF16)
        onesK = sbuf("onesK", [128, 1], BF16)
        ones1 = sbuf("ones1", [1, 128], BF16)


        def run(E):
            real = not E.dry

            CHAINKEY = {"vector": "V", "scalar": "S"}
            ENG_ORDER = ("sync", "gpsimd", "tensor", "scalar", "vector")
            pending = {n: [] for n in ENG_ORDER}

            def block(*bodies):
                for name, fn in bodies:
                    pending[name].append(fn)

            def flush_program():
                def run_engine(name, eng):
                    E.chain = CHAINKEY.get(name)
                    for fn in pending[name]:
                        fn(eng)
                    E.chain = None
                if E.dry:
                    for name in ENG_ORDER:
                        run_engine(name, None)
                else:
                    with nc.Block() as blk:
                        for name in ENG_ORDER:
                            getattr(blk, name)(
                                lambda eng, name=name: run_engine(name, eng))

            rs = ExitStack()

            def ralloc(name, shape, dt):
                if real:
                    return rs.enter_context(
                        nc.sbuf_tensor(name, shape, dt, side="right"))
                return None

            def lalloc(stack, name, shape, dt):
                if real:
                    return stack.enter_context(
                        nc.sbuf_tensor(name, shape, dt, side="left"))
                return None

            def palloc(stack, name, shape, dt=F32):
                if real:
                    return stack.enter_context(
                        nc.psum_tensor(name, shape, dt))
                return None

            x_halo_ctx = ExitStack()
            h_ctx = ExitStack()
            x_halo = None
            h_buf = None
            tabf_ctx = ExitStack()
            kvtab_ctx = ExitStack()
            cosk = sink = None
            if real:
                h_buf = h_ctx.enter_context(
                    nc.sbuf_tensor("h_buf", [128, CH, TT], BF16, side="right"))
                cosk = kvtab_ctx.enter_context(
                    nc.sbuf_tensor("cosk", [128, TT], BF16, side="right"))
                sink = kvtab_ctx.enter_context(
                    nc.sbuf_tensor("sink", [128, TT], BF16, side="right"))
                x_halo = x_halo_ctx.enter_context(
                    nc.sbuf_tensor("x_halo", [128, CH, TH], F32, side="right"))
            def p0_sync(sync):
                for gr in range(4):
                    r0, r1 = gr * 512, gr * 512 + 512
                    E.op(lambda r0=r0, r1=r1, gr=gr: sync.dma_start(
                        out=x_halo[:, gr * 4:gr * 4 + 4, :],
                        in_=xT_d[r0:r1, 0:TH].rearrange(
                            "(k p) t -> p k t", p=128)),
                        f"W{gr}", 16, event=f"ld_xh_{gr}")
                    E.op(lambda r0=r0, r1=r1, gr=gr: sync.dma_start(
                        out=x_own[:, gr * 4:gr * 4 + 4, :],
                        in_=xT_d[r0:r1, TH:TT].rearrange(
                            "(k p) t -> p k t", p=128)),
                        f"W{gr + 4}", 16, event=f"ld_xo_{gr}")
                E.op(lambda: sync.dma_start(out=cosk[:], in_=cos_d[:, :]),
                     "X", 16)
                E.op(lambda: sync.dma_start(out=sink[:], in_=sin_d[:, :]),
                     "X", 16)
                E.op(lambda: sync.dma_start(out=cosq[:],
                                            in_=cos_d[:, TH:TT]),
                     "X", 16)
                E.op(lambda: sync.dma_start(out=sinq[:],
                                            in_=sin_d[:, TH:TT]),
                     "X", 16)
                E.op(lambda: sync.dma_start(out=lam_sb[:], in_=lam_d[:, :]),
                     "X", 16, event="ld_lam")
                E.op(lambda: sync.dma_start(out=ident[:], in_=id_d[:, :]),
                     "X", 16, event="ld_id")

            def p0_vector(v):
                E.op(lambda: v.memset(onesK[:], 1.0))
                E.op(lambda: v.memset(ones1[:], 1.0))

            block(("sync", p0_sync), ("vector", p0_vector))

            def emit_ln(tag, regions, h_out, Ttot, pre_wait, packed=False,
                        x_wait=None):
                ln_ctx = ExitStack()
                xb_t = None
                if real:
                    xb_t = ln_ctx.enter_context(nc.sbuf_tensor(
                        tag + "_xb", [128, 2, Ttot], BF16, side="right"))
                    xq_t = ln_ctx.enter_context(nc.sbuf_tensor(
                        tag + "_xq", [128, 2, Ttot], BF16, side="right"))
                    m_row = ln_ctx.enter_context(nc.sbuf_tensor(
                        tag + "_m", [1, Ttot], F32, side="right"))
                    w_row = ln_ctx.enter_context(nc.sbuf_tensor(
                        tag + "_w", [1, Ttot], F32, side="right"))
                    a16 = ln_ctx.enter_context(nc.sbuf_tensor(
                        tag + "_a", [1, Ttot], BF16, side="right"))
                    b16 = ln_ctx.enter_context(nc.sbuf_tensor(
                        tag + "_b", [1, Ttot], BF16, side="right"))
                else:
                    xq_t = m_row = w_row = a16 = b16 = None
                st_ctx = ExitStack()
                sum_ps = palloc(st_ctx, tag + "_sum", [1, Ttot])
                sq_ps = palloc(st_ctx, tag + "_sq", [1, Ttot])

                def v_body(v):
                    pre_wait(v)
                    for k in range(CH):
                        if k >= 2:
                            E.wait_ev(v, "T", f"{tag}_st_{k-2}")
                            E.wait_ev(v, "G", f"{tag}_gsq_{k-2}")
                        if x_wait is not None:
                            x_wait(v, k)
                        for ri, (xb, off, ncol) in enumerate(regions):
                            E.op(lambda xb=xb, k=k, off=off, ncol=ncol:
                                 v.tensor_copy(
                                     xb_t[:, k % 2, off:off + ncol],
                                     xb()[:, k, :]),
                                 "V" if ri == len(regions) - 1 else None, 1,
                                 event=(f"{tag}_cast_{k}"
                                        if ri == len(regions) - 1 else None))
                    E.wait_ev(v, "T", f"{tag}_st_{CH-1}")
                    E.op(lambda: v.tensor_scalar(
                        m_row[:], sum_ps[:], 1.0 / 2048.0, None, ALU.mult))
                    E.op(lambda: v.scalar_tensor_tensor(
                        w_row[:], m_row[:], -1.0, m_row[:],
                        ALU.mult, ALU.mult))
                    E.op(lambda: v.scalar_tensor_tensor(
                        w_row[:], sq_ps[:], 1.0 / 2048.0, w_row[:],
                        ALU.mult, ALU.add))
                    E.op(lambda: v.tensor_scalar(
                        w_row[:], w_row[:], 1e-5, None, ALU.add),
                        "V", 1, event=f"{tag}_var")
                    E.wait_ev(v, "S", f"{tag}_rstd")
                    E.op(lambda: v.tensor_copy(a16[:], w_row[:]))
                    E.op(lambda: v.scalar_tensor_tensor(
                        b16[:], m_row[:], -1.0, w_row[:], ALU.mult, ALU.mult),
                        "V", 1, event=f"{tag}_ab")

                def gsq_body(g):
                    pre_wait(g)
                    for k in range(CH):
                        E.wait_ev(g, "V", f"{tag}_cast_{k}")
                        if k >= 2:
                            E.wait_ev(g, "T", f"{tag}_st_{k-2}")
                        for ri, (xb, off, ncol) in enumerate(regions):
                            E.op(lambda k=k, off=off, ncol=ncol:
                                 g.tensor_tensor(
                                     xq_t[:, k % 2, off:off + ncol],
                                     xb_t[:, k % 2, off:off + ncol],
                                     xb_t[:, k % 2, off:off + ncol],
                                     ALU.mult),
                                 "G" if ri == len(regions) - 1 else None, 1,
                                 event=(f"{tag}_gsq_{k}"
                                        if ri == len(regions) - 1 else None))

                def s_body(s):
                    pre_wait(s)
                    E.wait_ev(s, "V", f"{tag}_var")
                    E.op(lambda: s.activation(w_row[:], w_row[:], AF.Ln))
                    E.op(lambda: s.activation(w_row[:], w_row[:], AF.Exp,
                                              scale=-0.5),
                         "S", 1, event=f"{tag}_rstd")

                def t_body(t):
                    segs = []
                    for (xb, off, ncol) in regions:
                        for c0 in range(0, ncol, 512):
                            segs.append((off + c0, min(512, ncol - c0)))
                    for k in range(CH):
                        E.wait_ev(t, "V", f"{tag}_cast_{k}")
                        E.wait_ev(t, "G", f"{tag}_gsq_{k}")
                        for si, (s0, sn) in enumerate(segs):
                            E.op(lambda k=k, s0=s0, sn=sn: t.matmul(
                                sum_ps[0:1, s0:s0 + sn], onesK[:],
                                xb_t[:, k % 2, s0:s0 + sn],
                                start=(k == 0), stop=(k == CH - 1)))
                            last = si == len(segs) - 1
                            E.op(lambda k=k, s0=s0, sn=sn: t.matmul(
                                sq_ps[0:1, s0:s0 + sn], onesK[:],
                                xq_t[:, k % 2, s0:s0 + sn],
                                start=(k == 0), stop=(k == CH - 1)),
                                "T" if last else None, 1,
                                event=(f"{tag}_st_{k}" if last else None))

                block(("gpsimd", gsq_body),
                      ("tensor", t_body), ("scalar", s_body),
                      ("vector", v_body))
                if real:
                    st_ctx.close()
                ab_ctx = ExitStack()
                A_ps = palloc(ab_ctx, tag + "_A", [128, Ttot])
                B_ps = palloc(ab_ctx, tag + "_B", [128, Ttot])

                def t2_body(t):
                    E.wait_ev(t, "V", f"{tag}_ab")
                    nseg = (Ttot + 511) // 512
                    for ci in range(nseg):
                        c0, c1 = ci * 512, min(ci * 512 + 512, Ttot)
                        E.op(lambda c0=c0, c1=c1: t.matmul(
                            A_ps[:, c0:c1], ones1[:], a16[:, c0:c1],
                            start=True, stop=True))
                        E.op(lambda c0=c0, c1=c1: t.matmul(
                            B_ps[:, c0:c1], ones1[:], b16[:, c0:c1],
                            start=True, stop=True),
                            "T" if ci == nseg - 1 else None, 1,
                            event=(f"{tag}_bc" if ci == nseg - 1 else None))

                W4 = Ttot // 4

                def pk_dst(k, off, ncol):
                    return h_out()[:, k, :].rearrange(
                        "p (e q) -> p e q", q=W4)[:, :,
                                                  off // 4:off // 4 + ncol // 4]

                def pk_src(ap):
                    return ap.rearrange("p (u e) -> p e u", e=4)

                def v2_body(v):
                    E.wait_ev(v, "T", f"{tag}_bc")
                    E.op(lambda: v.tensor_copy(A_sb[:, 0:Ttot],
                                               A_ps[:, 0:Ttot]))
                    E.op(lambda: v.tensor_copy(B_sb[:, 0:Ttot],
                                               B_ps[:, 0:Ttot]),
                         "V", 1, event=f"{tag}_absb")
                    for k in range(CH):
                        for ri, (xb, off, ncol) in enumerate(regions):
                            lastr = ri == len(regions) - 1
                            ev = f"{tag}_h_{k}" if lastr else None
                            if packed:
                                E.op(lambda xb=xb, k=k, off=off, ncol=ncol:
                                     v.tensor_tensor(
                                         pk_dst(k, off, ncol),
                                         pk_src(xb()[:, k, :]),
                                         pk_src(A_sb[:, off:off + ncol]),
                                         ALU.mult))
                                E.op(lambda k=k, off=off, ncol=ncol:
                                     v.tensor_tensor(
                                         pk_dst(k, off, ncol),
                                         pk_dst(k, off, ncol),
                                         pk_src(B_sb[:, off:off + ncol]),
                                         ALU.add),
                                     "V" if lastr else None, 1, event=ev)
                                continue
                            E.op(lambda xb=xb, k=k, off=off, ncol=ncol:
                                 v.tensor_tensor(
                                     h_out()[:, k, off:off + ncol],
                                     xb()[:, k, :],
                                     A_sb[:, off:off + ncol], ALU.mult))
                            E.op(lambda k=k, off=off, ncol=ncol:
                                 v.tensor_tensor(
                                     h_out()[:, k, off:off + ncol],
                                     h_out()[:, k, off:off + ncol],
                                     B_sb[:, off:off + ncol], ALU.add),
                                 "V" if lastr else None, 1, event=ev)
                    if E.dry:
                        E.rec[f"{tag}_done"] = E.rec[f"{tag}_h_{CH-1}"]

                block(("tensor", t2_body), ("vector", v2_body))
                if real:
                    ab_ctx.close()
                    ln_ctx.close()

            def emit_ffn(tag, h_get, Ttot, iw, res_splits, write_cache,
                         read_cache, wring_get, h_tag, first_waits=None,
                         pre_gi=None):
                ffn_ctx = ExitStack()
                if real:
                    silu_t = ffn_ctx.enter_context(nc.sbuf_tensor(
                        tag + "_silu", [128, 2, iw], BF16, side="right"))
                    inter = ffn_ctx.enter_context(nc.sbuf_tensor(
                        tag + "_int", [128, CH, iw], BF16, side="right"))
                else:
                    silu_t = inter = None
                pg = palloc(ffn_ctx, tag + "_pg", [128, 2, 512])
                pu = palloc(ffn_ctx, tag + "_pu", [128, 2, 512])
                pd = palloc(ffn_ctx, tag + "_pd", [128, 2, 512])

                def is_down(gi):
                    return gi % 48 >= 32

                def g_body(g):
                    gi = 0
                    for e in range(NE):
                        for i in range(32):
                            m, is_up = i // 2, i % 2
                            col0 = (2048 if is_up else 0) + m * 128
                            if gi == 0 and first_waits is not None:
                                first_waits(g)
                            if gi >= NSLOT:
                                E.wait_ev(g, "T", f"{tag}_ev_{gi-NSLOT}")
                                if write_cache and is_down(gi - NSLOT):
                                    E.wait_ev(g, f"C{gi % NSLOT}",
                                              f"{tag}_c_{gi-NSLOT}")
                            sl = gi % NSLOT
                            if pre_gi is not None and gi < pre_gi:
                                gi += 1
                                continue
                            E.op(lambda e=e, i=i, sl=sl:
                                 g.dma_start(
                                     out=wring_get()[:, sl, :],
                                     in_=wg_d[e, i]),
                                 f"W{sl}" if gi >= NSLOT else f"P{sl}",
                                 16, event=f"{tag}_w_{gi}")
                            gi += 1
                        for m2 in range(16):
                            if gi >= NSLOT:
                                E.wait_ev(g, "T", f"{tag}_ev_{gi-NSLOT}")
                                if write_cache and is_down(gi - NSLOT):
                                    E.wait_ev(g, f"C{gi % NSLOT}",
                                              f"{tag}_c_{gi-NSLOT}")
                            sl = gi % NSLOT
                            if read_cache:
                                E.op(lambda e=e, m2=m2, sl=sl: g.dma_start(
                                    out=wring_get()[:, sl, :],
                                    in_=wdc_d[e, m2]),
                                    f"W{sl}", 16, event=f"{tag}_w_{gi}")
                            else:
                                E.op(lambda e=e, m2=m2, sl=sl: g.dma_start(
                                    out=wring_get()[:, sl, :],
                                    in_=wd_d[e, m2]),
                                    f"W{sl}", 16, event=f"{tag}_w_{gi}")
                            gi += 1

                def sync_body(sync):
                    if not write_cache:
                        return
                    gi = 0
                    for e in range(NE):
                        gi += 32
                        for m2 in range(16):
                            sl = gi % NSLOT
                            E.wait_ev(sync, f"W{sl}", f"{tag}_w_{gi}")
                            E.op(lambda e=e, m2=m2, sl=sl: sync.dma_start(
                                out=wdc_d[e, m2], in_=wring_get()[:, sl, :]),
                                f"C{sl}", 16, event=f"{tag}_c_{gi}")
                            gi += 1

                def t_body(t):
                    gi = 0
                    for e in range(NE):
                        for i in range(32):
                            m, is_up = i // 2, i % 2
                            ps = pu if is_up else pg
                            E.wait_ev(t, ("P" if pre_gi is not None
                                          and gi < pre_gi else "W")
                                      + str(gi % NSLOT), f"{tag}_w_{gi}")
                            if is_up:
                                if e * 16 + m >= 2:
                                    E.wait_ev(t, "V",
                                              f"{tag}_int_{e*16+m-2}")
                            else:
                                if e * 16 + m >= 2:
                                    E.wait_ev(t, "S",
                                              f"{tag}_silu_{e*16+m-2}")
                            sl = gi % NSLOT
                            for k in range(CH):
                                if gi == 0:
                                    E.wait_ev(t, "V", f"{h_tag}_h_{k}")
                                E.op(lambda ps=ps, m=m, k=k, sl=sl, e=e:
                                     t.matmul(
                                         ps[:, m % 2, 0:iw],
                                         wring_get()[:, sl,
                                                     k * 128:(k + 1) * 128],
                                         h_get()[:, k, e * iw:(e + 1) * iw],
                                         start=(k == 0), stop=(k == CH - 1)),
                                     "T" if k == CH - 1 else None, 1,
                                     event=(f"{tag}_ev_{gi}"
                                            if k == CH - 1 else None))
                            gi += 1
                        for m2 in range(16):
                            E.wait_ev(t, f"W{gi % NSLOT}", f"{tag}_w_{gi}")
                            if m2 == 0:
                                E.wait_ev(t, "V", f"{tag}_int_{e*16+15}")
                            if e * 16 + m2 >= 2:
                                E.wait_ev(t, "V", f"{tag}_res_{e*16+m2-2}")
                            sl = gi % NSLOT
                            for k in range(CH):
                                E.op(lambda m2=m2, k=k, sl=sl:
                                     t.matmul(
                                         pd[:, m2 % 2, 0:iw],
                                         wring_get()[:, sl,
                                                     k * 128:(k + 1) * 128],
                                         inter[:, k, :],
                                         start=(k == 0), stop=(k == CH - 1)),
                                     "T" if k == CH - 1 else None, 1,
                                     event=(f"{tag}_ev_{gi}"
                                            if k == CH - 1 else None))
                            gi += 1

                def s_body(s):
                    for e in range(NE):
                        for m in range(16):
                            gi = e * 48 + 2 * m
                            E.wait_ev(s, "T", f"{tag}_ev_{gi}")
                            if e * 16 + m >= 2:
                                E.wait_ev(s, "V", f"{tag}_int_{e*16+m-2}")
                            E.op(lambda m=m: s.activation(
                                silu_t[:, m % 2, :], pg[:, m % 2, 0:iw],
                                AF.Silu),
                                "S", 1, event=f"{tag}_silu_{e*16+m}")

                def v_body(v):
                    for e in range(NE):
                        for m in range(16):
                            gi = e * 48 + 2 * m + 1
                            E.wait_ev(v, "T", f"{tag}_ev_{gi}")
                            E.wait_ev(v, "S", f"{tag}_silu_{e*16+m}")
                            E.op(lambda m=m: v.tensor_tensor(
                                inter[:, m, :], silu_t[:, m % 2, :],
                                pu[:, m % 2, 0:iw], ALU.mult),
                                "V", 1, event=f"{tag}_int_{e*16+m}")
                        for m2 in range(16):
                            gi = e * 48 + 32 + m2
                            E.wait_ev(v, "T", f"{tag}_ev_{gi}")
                            for ri, (xg, pc0, ncol) in enumerate(res_splits):
                                last = ri == len(res_splits) - 1
                                E.op(lambda xg=xg, m2=m2, pc0=pc0, ncol=ncol,
                                     e=e: v.scalar_tensor_tensor(
                                         xg()[:, m2, e::4],
                                         pd[:, m2 % 2, pc0:pc0 + ncol], 0.5,
                                         xg()[:, m2, e::4],
                                         ALU.mult, ALU.add),
                                     "V" if last else None, 1,
                                     event=(f"{tag}_res_{e*16+m2}"
                                            if last else None))

                block(("gpsimd", g_body), ("sync", sync_body),
                      ("tensor", t_body), ("scalar", s_body),
                      ("vector", v_body))
                if real:
                    ffn_ctx.close()

            def pre_ln1(eng):
                pass

            def ln1_x_wait(eng, k):
                E.wait_ev(eng, f"W{k // 4}", f"ld_xh_{k // 4}")
                E.wait_ev(eng, f"W{k // 4 + 4}", f"ld_xo_{k // 4}")

            ffn1_ring_ctx = ExitStack()
            wring1 = None
            if real:
                wring1 = ffn1_ring_ctx.enter_context(nc.sbuf_tensor(
                    "wring1", [128, NSLOT, CH * 128], BF16, side="right"))

            def ffn1_preload(g):
                for gi in range(NSLOT):
                    e, i = divmod(gi, 32)
                    m, is_up = i // 2, i % 2
                    E.op(lambda e=e, i=i, gi=gi: g.dma_start(
                        out=wring1[:, gi, :], in_=wg_d[e, i]),
                        f"P{gi}", 16, event=f"ffn1_w_{gi}")

            block(("gpsimd", ffn1_preload))

            emit_ln("ln1", [(lambda: x_halo, 0, TH), (lambda: x_own, TH, TO)],
                    lambda: h_buf, TT, pre_ln1, packed=True,
                    x_wait=ln1_x_wait)

            emit_ffn("ffn1", lambda: h_buf, TT, 384,
                     [(lambda: x_halo, 0, 128), (lambda: x_own, 128, 256)],
                     write_cache=True, read_cache=False,
                     wring_get=lambda: wring1, h_tag="ln1",
                     first_waits=None, pre_gi=NSLOT)
            if real:
                ffn1_ring_ctx.close()

            def pre_ln2(eng):
                E.wait_ev(eng, "V", f"ffn1_res_{NE*16-1}")
                E.wait_ev(eng, "T", f"ffn1_ev_{NE*48-1}")
                for sl in range(NSLOT):
                    E.wait_ev(eng, f"C{sl}", f"ffn1_c_{NE*48-NSLOT+sl}")

            emit_ln("ln2", [(lambda: x_halo, 0, TH), (lambda: x_own, TH, TO)],
                    lambda: h_buf, TT, pre_ln2)

            if stage == "ffn1":
                def out_sync(sync):
                    E.wait_ev(sync, "V", "ln2_done")
                    E.op(lambda: sync.dma_start(
                        out=yT_d[:, :].rearrange("(k p) t -> p k t", p=128),
                        in_=x_own[:, :, :]),
                        "O", 16)
                    E.wait_val(sync, "O", 16)
                block(("sync", out_sync))
                flush_program()
                return

            attn_left_ctx = ExitStack()
            ob = kb = vbuf = None
            if real:
                kb = attn_left_ctx.enter_context(nc.sbuf_tensor(
                    "kb", [128, NKV, TT], BF16, side="left"))
                vbuf = attn_left_ctx.enter_context(nc.sbuf_tensor(
                    "vbuf", [128, 12, 512], BF16, side="left"))
            kv_ctx2 = ExitStack()
            wkvb = tbk = None
            if real:
                wkvb = kv_ctx2.enter_context(nc.sbuf_tensor(
                    "wkvb", [128, CH, 512], BF16, side="right"))
                tbk = kv_ctx2.enter_context(nc.sbuf_tensor(
                    "tbk", [128, TT], BF16, side="right"))
            pskv_ctx = ExitStack()
            ps_k = palloc(pskv_ctx, "ps_k", [128, 2, TT])

            def kv_g(g):
                E.wait_ev(g, "V", "ln2_ab")
                E.wait_ev(g, "T", "ln2_bc")
                E.op(lambda: g.dma_start(
                    out=wkvb[:, :, :], in_=wk_d[:, :, :]),
                    "W", 16, event="w_wk")
                E.wait_ev(g, "T", "kmm_3")
                E.op(lambda: g.dma_start(
                    out=wkvb[:, :, :], in_=wv_d[:, :, :]),
                    "W", 16, event="w_wv")

            def kv_t(t):
                E.wait_ev(t, "W", "w_wk")
                for m in range(NKV):
                    if m >= 2:
                        E.wait_ev(t, "V", f"krope_{m-2}")
                    for ci in range(3):
                        c0, c1 = ci * 512, ci * 512 + 512
                        for k in range(CH):
                            if m == 0 and ci == 0:
                                E.wait_ev(t, "V", f"ln2_h_{k}")
                            last = (ci == 2 and k == CH - 1)
                            E.op(lambda m=m, k=k, c0=c0, c1=c1: t.matmul(
                                ps_k[:, m % 2, c0:c1],
                                wkvb[:, k, m * 128:(m + 1) * 128],
                                h_buf[:, k, c0:c1],
                                start=(k == 0), stop=(k == CH - 1)),
                                "T" if last else None, 1,
                                event=(f"kmm_{m}" if last else None))
                E.wait_ev(t, "W", "w_wv")
                for tv in range(12):
                    if tv >= 2:
                        E.wait_ev(t, "S", f"vevac_{tv-2}")
                    else:
                        E.wait_ev(t, "V", f"krope_{NKV - 2 + tv}")
                    for k in range(CH):
                        last = (k == CH - 1)
                        E.op(lambda tv=tv, k=k: t.matmul(
                            ps_k[:, tv % 2, 0:512],
                            h_buf[:, k, tv * 128:(tv + 1) * 128],
                            wkvb[:, k, :],
                            start=(k == 0), stop=(k == CH - 1)),
                            "T" if last else None, 1,
                            event=(f"vmm_{tv}" if last else None))

            def kv_s(s):
                for m in range(NKV):
                    E.wait_ev(s, "T", f"kmm_{m}")
                    E.op(lambda m=m: s.copy(kb[:, m, :],
                                            ps_k[:, m % 2, 0:TT]),
                         "S", 1, event=f"kevac_{m}")
                for tv in range(12):
                    E.wait_ev(s, "T", f"vmm_{tv}")
                    E.op(lambda tv=tv: s.copy(vbuf[:, tv, :],
                                              ps_k[:, tv % 2, 0:512]),
                         "S", 1, event=f"vevac_{tv}")

            def kv_v(v):
                for m in range(NKV):
                    E.wait_ev(v, "S", f"kevac_{m}")
                    E.op(lambda m=m: v.tensor_tensor(
                        tbk[0:64, :], ps_k[64:128, m % 2, 0:TT],
                        sink[0:64, :], ALU.mult))
                    E.op(lambda m=m: v.tensor_tensor(
                        tbk[64:128, :], ps_k[0:64, m % 2, 0:TT],
                        sink[64:128, :], ALU.mult))
                    E.op(lambda m=m: v.tensor_tensor(
                        kb[:, m, :], kb[:, m, :], cosk[:], ALU.mult))
                    E.op(lambda m=m: v.tensor_tensor(
                        kb[:, m, :], kb[:, m, :], tbk[:, :], ALU.add),
                        "V", 1, event=f"krope_{m}")

            block(("gpsimd", kv_g), ("tensor", kv_t), ("scalar", kv_s),
                  ("vector", kv_v))
            if real:
                pskv_ctx.close()
                kv_ctx2.close()
                x_halo_ctx.close()
                kvtab_ctx.close()

            at_ctx = ExitStack()
            wqr = qh = tbq = e1 = e2 = bA = pt_sb = mask_sb = None
            if real:
                ob = attn_left_ctx.enter_context(nc.sbuf_tensor(
                    "ob", [128, NH, TO], BF16, side="left"))
            if real:
                wqr = at_ctx.enter_context(nc.sbuf_tensor(
                    "wqr", [128, 2, CH * 128], BF16, side="right"))
                qh = at_ctx.enter_context(nc.sbuf_tensor(
                    "qh", [128, 3, TO], BF16, side="right"))
                tbq = at_ctx.enter_context(nc.sbuf_tensor(
                    "tbq", [128, TO], BF16, side="right"))
                e1 = at_ctx.enter_context(nc.sbuf_tensor(
                    "e1", [128, 2, WK], BF16, side="right"))
                e2 = at_ctx.enter_context(nc.sbuf_tensor(
                    "e2", [128, 2, WK], BF16, side="right"))
                bA = at_ctx.enter_context(nc.sbuf_tensor(
                    "bA", [128, WK], BF16, side="right"))
                pt_sb = at_ctx.enter_context(nc.sbuf_tensor(
                    "pt_sb", [128, WK], BF16, side="right"))
                mask_sb = at_ctx.enter_context(nc.sbuf_tensor(
                    "mask_sb", [128, 5, 256], BF16, side="right"))
                mid_sb = at_ctx.enter_context(nc.sbuf_tensor(
                    "mid_sb", [128, 3, 384], BF16, side="right"))
                rows = {}
                rows["sc2"] = at_ctx.enter_context(nc.sbuf_tensor(
                    "r_sc2", [128, 2, 2], F32, side="right"))
                rows["iv"] = at_ctx.enter_context(nc.sbuf_tensor(
                    "r_iv", [128, 2], F32, side="right"))
                rows["sd"] = at_ctx.enter_context(nc.sbuf_tensor(
                    "r_sd", [128, 1], F32, side="right"))
                rows["gam"] = at_ctx.enter_context(nc.sbuf_tensor(
                    "r_gam", [128, 2], F32, side="right"))
                rows["ivl"] = at_ctx.enter_context(nc.sbuf_tensor(
                    "r_ivl", [128, 2], F32, side="right"))
            else:
                rows = {rn: None for rn in ("sc2", "iv", "sd", "gam", "ivl")}
            ps_at_ctx = ExitStack()
            ps_s1 = palloc(ps_at_ctx, "ps_s1", [128, TO])
            ps_s2 = palloc(ps_at_ctx, "ps_s2", [128, TO])
            ps_pt = palloc(ps_at_ctx, "ps_pt", [128, TO], BF16)
            ps_oo = palloc(ps_at_ctx, "ps_oo", [128, 512])
            ps_q = palloc(ps_at_ctx, "ps_q", [128, TO])
            NIT = NH * NJ

            def at_sync(sync):
                E.wait_ev(sync, "T", "vmm_11")
                E.wait_ev(sync, "V", "krope_3")
                E.wait_ev(sync, "S", "vevac_11")
                E.op(lambda: sync.dma_start(
                    out=mask_sb[:],
                    in_=maskE_d[:, :, :].rearrange("j p c -> p j c")),
                    "X", 16)
                E.op(lambda: sync.dma_start(
                    out=mid_sb[:],
                    in_=maskM_d[:, :, :].rearrange("j p c -> p j c")),
                    "X", 16, event="ld_mask")

            def at_g(g):
                def emit_qdma(h):
                    if h == 0:
                        E.wait_ev(g, "T", "vmm_11")
                        E.wait_ev(g, "V", "krope_3")
                        E.wait_ev(g, "S", "vevac_11")
                    if h >= 2:
                        E.wait_ev(g, "T", f"qmm_{h-2}")
                    E.op(lambda h=h: g.dma_start(
                        out=wqr[:, h % 2, :], in_=wq_d[h]),
                        f"Q{h % 2}", 16, event=f"wqdma_{h}")

                emit_qdma(0)
                emit_qdma(1)
                for h in range(2, NH):
                    emit_qdma(h)

            def at_t(t):
                def q_mms(h):
                    E.wait_ev(t, f"Q{h % 2}", f"wqdma_{h}")
                    if h >= 1:
                        E.wait_ev(t, "V", f"qrope_{h-1}")
                    else:
                        E.wait_ev(t, "S", "vevac_11")
                        E.wait_ev(t, "V", "krope_3")
                        E.wait_ev(t, "X", "ld_id")
                    for ci in range(2):
                        c0, c1 = ci * 512, ci * 512 + 512
                        for k in range(CH):
                            last = (ci == 1 and k == CH - 1)
                            E.op(lambda h=h, k=k, c0=c0, c1=c1: t.matmul(
                                ps_q[:, c0:c1],
                                wqr[:, h % 2, k * 128:(k + 1) * 128],
                                h_buf[:, k, TH + c0:TH + c1],
                                start=(k == 0), stop=(k == CH - 1)),
                                "T" if last else None, 1,
                                event=(f"qmm_{h}" if last else None))

                def emit_scores(i):
                    h, j = i // NJ, i % NJ
                    kv = h // 4
                    c0 = j * 128
                    if j == 0:
                        E.wait_ev(t, "V", f"qrope_{h}")
                    for half in range(2):
                        if i >= 1:
                            E.wait_ev(t, "S", f"exp{half+1}_{i-1}")
                        d0 = half * 64
                        ps = ps_s1 if half == 0 else ps_s2
                        for si, (a0, a1) in enumerate(
                                [(0, 128), (128, 512), (512, 640)]):
                            do_mask = (si != 1) or (j <= 2)
                            E.op(lambda h=h, d0=d0, j=j, kv=kv, c0=c0,
                                 a0=a0, a1=a1, ps=ps, dm=do_mask: t.matmul(
                                     ps[:, a0:a1],
                                     qh[d0:d0 + 64, h % 3,
                                        j * 128:(j + 1) * 128],
                                     kb[d0:d0 + 64, kv,
                                        c0 + a0:c0 + a1],
                                     start=True, stop=not dm))
                            if do_mask:
                                last = (si == 2)
                                mj = min(j, 4)
                                if si == 0:
                                    msrc = lambda mj=mj: mask_sb[:, mj, 0:128]
                                elif si == 2:
                                    msrc = lambda mj=mj: mask_sb[:, mj,
                                                                 128:256]
                                else:
                                    msrc = lambda j=j: mid_sb[:, j, :]
                                E.op(lambda a0=a0, a1=a1, ps=ps, msrc=msrc:
                                     t.matmul(
                                         ps[:, a0:a1], ident[:],
                                         msrc(),
                                         start=False, stop=True),
                                     "T" if last else None, 1,
                                     event=(f"sc{half+1}_{i}"
                                            if last else None))

                q_mms(0)
                emit_scores(0)
                q_mms(1)
                emit_scores(1)
                for i in range(NIT):
                    h, j = i // NJ, i % NJ
                    kv = h // 4
                    E.wait_ev(t, "V", f"p_{i}")
                    if i >= 1:
                        E.wait_ev(t, "S", f"ptev_{i-1}")
                    for c in range(5):
                        E.op(lambda c=c, i=i: t.transpose(
                            ps_pt[:, c * 128:(c + 1) * 128],
                            bA[:, c * 128:(c + 1) * 128],
                            ident[:]),
                            "T" if c == 4 else None, 1,
                            event=(f"tr_{i}" if c == 4 else None))
                    if i >= 2 and (i - 2) % NJ == 0 and i // NJ + 2 < NH:
                        q_mms(i // NJ + 2)
                    if i + 2 < NIT:
                        emit_scores(i + 2)
                    E.wait_ev(t, "S", f"ptev_{i}")
                    if i >= 1:
                        E.wait_ev(t, "V", f"oev_{i-1}")
                    for c in range(5):
                        tv = j + c
                        E.op(lambda tv=tv, kv=kv, c=c: t.matmul(
                            ps_oo[:, 0:128],
                            vbuf[:, tv, kv * 128:(kv + 1) * 128],
                            pt_sb[:, c * 128:(c + 1) * 128],
                            start=(c == 0), stop=(c == 4)),
                            "T" if c == 4 else None, 1,
                            event=(f"pvd_{i}" if c == 4 else None))

            def at_s(s):
                def emit_qevac(h):
                    E.wait_ev(s, "T", f"qmm_{h}")
                    if h >= 3:
                        E.wait_ev(s, "T", f"sc2_{(h-3)*NJ + NJ - 1}")
                    E.op(lambda h=h: s.copy(qh[:, h % 3, :], ps_q[:, 0:TO]),
                         "S", 1, event=f"qevac_{h}")

                def emit_exp(i):
                    E.wait_ev(s, "T", f"sc1_{i}")
                    if i >= 2:
                        E.wait_ev(s, "V", f"p_{i-2}")
                    E.op(lambda i=i: s.activation(
                        e1[:, i % 2, :], ps_s1[:, 0:WK], AF.Exp, scale=0.125,
                        accum_out=rows["sc2"][:, i % 2, 0:1]),
                        "S", 1, event=f"exp1_{i}")
                    E.wait_ev(s, "T", f"sc2_{i}")
                    E.op(lambda i=i: s.activation(
                        e2[:, i % 2, :], ps_s2[:, 0:WK], AF.Exp, scale=0.125,
                        accum_out=rows["sc2"][:, i % 2, 1:2]),
                         "S", 1, event=f"exp2_{i}")

                emit_qevac(0)
                emit_qevac(1)
                for i in range(NIT):
                    if i % NJ == 4 and i // NJ + 2 < NH:
                        emit_qevac(i // NJ + 2)
                    if i >= 2:
                        _at_s_tail(s, i - 2)
                    emit_exp(i)
                _at_s_tail(s, NIT - 2)
                _at_s_tail(s, NIT - 1)

            def _at_s_tail(s, i):
                E.wait_ev(s, "T", f"tr_{i}")
                E.op(lambda i=i: s.copy(pt_sb[:], ps_pt[:, 0:WK]),
                     "S", 1, event=f"ptev_{i}")

            def at_v(v):
                def emit_rope(h):
                    E.wait_ev(v, "S", f"qevac_{h}")
                    E.op(lambda h=h: v.tensor_tensor(
                        tbq[0:64, :], ps_q[64:128, 0:TO],
                        sinq[0:64, :], ALU.mult))
                    E.op(lambda h=h: v.tensor_tensor(
                        tbq[64:128, :], ps_q[0:64, 0:TO],
                        sinq[64:128, :], ALU.mult))
                    E.op(lambda h=h: v.tensor_tensor(
                        qh[:, h % 3, :], qh[:, h % 3, :],
                        cosq[:], ALU.mult))
                    E.op(lambda h=h: v.tensor_tensor(
                        qh[:, h % 3, :], qh[:, h % 3, :], tbq[:, :],
                        ALU.add),
                        "V", 1, event=f"qrope_{h}")

                E.wait_ev(v, "X", "ld_mask")
                emit_rope(0)
                emit_rope(1)
                for i in range(NIT):
                    if i % NJ == 0 and 2 <= i // NJ + 1 < NH:
                        emit_rope(i // NJ + 1)
                    _at_v_iter(v, i // NJ, i % NJ)
                _at_v_oev(v, NIT - 2)
                _at_v_oev(v, NIT - 1)

            def _at_v_iter(v, h, j):
                i = h * NJ + j
                E.wait_ev(v, "S", f"exp2_{i}")
                E.op(lambda i=i: v.reciprocal(rows["iv"][:],
                                              rows["sc2"][:, i % 2, :]))
                E.op(lambda h=h, i=i: v.tensor_tensor(
                    rows["ivl"][:, i % 2:i % 2 + 1], rows["iv"][:, 1:2],
                    lam_sb[:, h:h + 1], ALU.mult))
                E.op(lambda i=i: v.tensor_scalar(
                    e2[:, i % 2, :], e2[:, i % 2, :],
                    rows["ivl"][:, i % 2:i % 2 + 1], None, ALU.mult))
                E.op(lambda i=i: v.scalar_tensor_tensor(
                    e1[:, i % 2, :], e1[:, i % 2, :], rows["iv"][:, 0:1],
                    e2[:, i % 2, :], ALU.mult, ALU.subtract))
                E.op(lambda i=i: v.tensor_scalar(
                    e1[:, i % 2, :], e1[:, i % 2, :], 0.0, 0.0,
                    ALU.max, ALU.add, accum_out=rows["sd"][:]))
                E.op(lambda: v.tensor_scalar(
                    rows["sd"][:], rows["sd"][:], 1e-6, None, ALU.add))
                E.op(lambda i=i: v.reciprocal(rows["gam"][:, i % 2:i % 2 + 1],
                                              rows["sd"][:]))
                if i >= 1:
                    E.wait_ev(v, "T", f"tr_{i-1}")
                E.op(lambda i=i: v.tensor_scalar(
                    bA[:], e1[:, i % 2, :], rows["gam"][:, i % 2:i % 2 + 1],
                    None, ALU.mult),
                    "V", 1, event=f"p_{i}")
                if i >= 2:
                    _at_v_oev(v, i - 2)

            def _at_v_oev(v, i):
                h, j = i // NJ, i % NJ
                E.wait_ev(v, "T", f"pvd_{i}")
                E.op(lambda h=h, j=j: v.tensor_copy(
                    ob[:, h, j * 128:(j + 1) * 128], ps_oo[:, 0:128]),
                    "V", 1, event=f"oev_{i}")

            block(("sync", at_sync), ("gpsimd", at_g), ("tensor", at_t),
                  ("scalar", at_s), ("vector", at_v))
            if real:
                ps_at_ctx.close()

            ps_wo_ctx = ExitStack()
            ps_wo = palloc(ps_wo_ctx, "ps_wo", [128, 2, TO])

            def wo_g(g):
                for m in range(CH):
                    if m < 2:
                        E.wait_ev(g, "T", f"qmm_{NH-1}")
                    else:
                        E.wait_ev(g, "T", f"womm_{m-2}")
                    E.op(lambda m=m: g.dma_start(
                        out=wqr[:, m % 2, :], in_=wo_d[m]),
                        f"Q{m % 2}", 16, event=f"wodma_{m}")

            def wo_t(t):
                E.wait_ev(t, "V", f"oev_{NH*NJ-1}")
                for m in range(CH):
                    E.wait_ev(t, f"Q{m % 2}", f"wodma_{m}")
                    if m >= 2:
                        E.wait_ev(t, "V", f"wores_{m-2}")
                    for ci in range(2):
                        c0, c1 = ci * 512, ci * 512 + 512
                        for k in range(CH):
                            last = (ci == 1 and k == CH - 1)
                            E.op(lambda m=m, k=k, c0=c0, c1=c1: t.matmul(
                                ps_wo[:, m % 2, c0:c1],
                                wqr[:, m % 2, k * 128:(k + 1) * 128],
                                ob[:, k, c0:c1],
                                start=(k == 0), stop=(k == CH - 1)),
                                "T" if last else None, 1,
                                event=(f"womm_{m}" if last else None))

            def wo_v(v):
                for m in range(CH):
                    E.wait_ev(v, "T", f"womm_{m}")
                    E.op(lambda m=m: v.tensor_tensor(
                        x_own[:, m, :], x_own[:, m, :], ps_wo[:, m % 2, :],
                        ALU.add),
                        "V", 1, event=f"wores_{m}")

            block(("gpsimd", wo_g), ("tensor", wo_t), ("vector", wo_v))
            if real:
                ps_wo_ctx.close()
                at_ctx.close()
                h_ctx.close()
                attn_left_ctx.close()

            if stage == "attn":
                def out_sync2(sync):
                    E.wait_ev(sync, "V", f"wores_{CH-1}")
                    E.op(lambda: sync.dma_start(
                        out=yT_d[:, :].rearrange("(k p) t -> p k t", p=128),
                        in_=x_own[:, :, :]),
                        "O", 16)
                    E.wait_val(sync, "O", 16)
                block(("sync", out_sync2))
                flush_program()
                return

            ffn2_ctx = ExitStack()
            h3 = wring2 = None
            if real:
                h3 = ffn2_ctx.enter_context(nc.sbuf_tensor(
                    "h3", [128, CH, TO], BF16, side="right"))
                wring2 = ffn2_ctx.enter_context(nc.sbuf_tensor(
                    "wring2", [128, NSLOT, CH * 128], BF16, side="right"))

            def pre_ln3(eng):
                E.wait_ev(eng, "V", f"wores_{CH-1}")
                E.wait_ev(eng, "T", f"womm_{CH-1}")
                E.wait_ev(eng, "V", f"oev_{NH*NJ-1}")

            emit_ln("ln3", [(lambda: x_own, 0, TO)], lambda: h3, TO, pre_ln3,
                    packed=True)
            def ffn2_first(g):
                E.wait_ev(g, "T", f"womm_{CH-1}")

            emit_ffn("ffn2", lambda: h3, TO, 256,
                     [(lambda: x_own, 0, 256)],
                     write_cache=False, read_cache=True,
                     wring_get=lambda: wring2, h_tag="ln3",
                     first_waits=ffn2_first)
            if real:
                ffn2_ctx.close()

            def out_sync3(sync):
                E.wait_ev(sync, "V", f"ffn2_res_{NE*16-1}")
                E.op(lambda: sync.dma_start(
                    out=yT_d[:, :].rearrange("(k p) t -> p k t", p=128),
                    in_=x_own[:, :, :]),
                    "O", 16)
                E.wait_val(sync, "O", 16)

            block(("sync", out_sync3))
            flush_program()
            if real:
                rs.close()

        dryE = Em(dry=True)
        run(dryE)
        realE = Em(dry=False, sems=sems, events=dryE.rec)
        run(realE)
        for k in SEMKEYS:
            assert dryE.cnt[k] == realE.cnt[k], (k, dryE.cnt[k], realE.cnt[k])
    return nc



def _mk_tables(core):
    p0 = core * TO - TH
    t = (np.arange(TT, dtype=np.float64) + p0).astype(np.float32)
    inv = (1.0 / (10000.0 ** (np.arange(0, 128, 2, dtype=np.float32)
                              / 128.0))).astype(np.float32)
    ang = t[None, :] * inv[:, None]
    ang = np.concatenate([ang, ang], axis=0)
    cos = np.cos(ang).astype(np.float32)
    sin = np.sin(ang).astype(np.float32)
    sinS = sin.copy()
    sinS[0:64] = -sin[0:64]
    return (np.ascontiguousarray(cos.astype(ml_dtypes.bfloat16)),
            np.ascontiguousarray(sinS.astype(ml_dtypes.bfloat16)))


def _mk_mask(core):
    j = np.arange(NJ)[:, None, None]
    r = np.arange(128)[None, :, None]
    k = np.arange(WK)[None, None, :]
    lq = TH + 128 * j + r
    lk = 128 * j + k
    gk = core * TO - TH + lk
    allowed = (lk >= lq - 511) & (lk <= lq) & (gk >= 0)
    m = np.where(allowed, 0.0, -1e9).astype(ml_dtypes.bfloat16)
    edge = np.zeros((5, 128, 256), dtype=ml_dtypes.bfloat16)
    for jj in range(5):
        jsrc = jj if jj < 4 else 4
        edge[jj, :, 0:128] = m[jsrc, :, 0:128]
        edge[jj, :, 128:256] = m[jsrc, :, 512:640]
    mid = np.ascontiguousarray(m[0:3, :, 128:512])
    return np.ascontiguousarray(edge), mid


def _pack_cols(w):
    nb = w.shape[1] // 128
    t = w.reshape(16, 128, nb, 128).transpose(2, 1, 0, 3)
    return np.ascontiguousarray(t.reshape(nb, 128, 16 * 128))


def kernel(L0_attn_x, L0_attn_Wq, L0_attn_Wk, L0_attn_Wv, L0_attn_Wo,
           L0_attn_lam, L0_ffn_gate_up_proj, L0_ffn_down_proj):
    x = np.ascontiguousarray(np.asarray(L0_attn_x, dtype=np.float32))
    wq = _pack_cols(np.asarray(L0_attn_Wq, dtype=np.float32))
    wk = np.ascontiguousarray(
        np.asarray(L0_attn_Wk, dtype=np.float32)
        .reshape(16, 128, 512).transpose(1, 0, 2))
    wv = np.ascontiguousarray(
        np.asarray(L0_attn_Wv, dtype=np.float32)
        .reshape(16, 128, 512).transpose(1, 0, 2))
    wo = _pack_cols(np.asarray(L0_attn_Wo, dtype=np.float32))
    lam = np.asarray(L0_attn_lam, dtype=np.float32)
    wg_raw = np.asarray(L0_ffn_gate_up_proj, dtype=np.float32)
    wd_raw = np.asarray(L0_ffn_down_proj, dtype=np.float32)
    gu_order = [16 * (i % 2) + i // 2 for i in range(32)]
    wg = np.stack([_pack_cols(wg_raw[e])[gu_order] for e in range(NE)])
    wd = np.stack([_pack_cols(wd_raw[e]) for e in range(NE)])

    xT = np.ascontiguousarray(x[0].T)
    lam_rep = np.ascontiguousarray(
        np.broadcast_to(lam[None, :], (128, NH)).astype(np.float32))
    idm = np.ascontiguousarray(np.eye(128, dtype=ml_dtypes.bfloat16))

    nc = build_nc(_STAGE)
    in_maps = []
    for c in range(NCORE):
        lo = c * TO - TH
        if lo < 0:
            xc = np.zeros((2048, TT), dtype=np.float32)
            xc[:, TH:] = xT[:, 0:TO]
        else:
            xc = np.ascontiguousarray(xT[:, lo:lo + TT])
        cos, sin = _mk_tables(c)
        edge, mid = _mk_mask(c)
        in_maps.append({
            "xT": xc, "wq": wq, "wk": wk, "wv": wv, "wo": wo,
            "wg": wg, "wd": wd, "lam": lam_rep, "cosT": cos, "sinT": sin,
            "maskE": edge, "maskM": mid, "identT": idm,
        })
    res = run_bass_kernel_spmd(nc, in_maps, core_ids=list(range(NCORE)))
    parts = [np.asarray(res.results[c]["yT"], dtype=np.float32).T
             for c in range(NCORE)]
    out = np.concatenate(parts, axis=0).reshape(1, 8192, 2048)
    return np.ascontiguousarray(out)


# version 21 (final)
# speedup vs baseline: 1.0842x; 1.0842x over previous
import sys
import numpy as np
import ml_dtypes
from contextlib import ExitStack

for _p in ("/opt/trn_rl_repo",):
    if _p not in sys.path:
        sys.path.insert(0, _p)

import concourse.bass as bass
import concourse.mybir as mybir
from concourse.bass_utils import run_bass_kernel_spmd

F32 = mybir.dt.float32
BF16 = mybir.dt.bfloat16
AF = mybir.ActivationFunctionType
ALU = mybir.AluOpType
AX = mybir.AxisListType

CH = 16
TH = 512
TO = 1024
TT = 1536
NE = 4
NH = 16
NKV = 4
NJ = 8
WK = 640
NSLOT = 8
NCORE = 8

_STAGE = "full"

SEMKEYS = (("X", "W", "T", "S", "V", "O", "G")
           + tuple(f"W{i}" for i in range(8))
           + tuple(f"C{i}" for i in range(8))
           + ("Q0", "Q1"))


class Em:

    def __init__(self, dry, sems=None, events=None):
        self.dry = dry
        self.sems = sems
        self.cnt = {k: 0 for k in SEMKEYS}
        self.rec = {} if dry else (events or {})
        self.chain = None

    def op(self, fn, inckey=None, amt=1, event=None):
        ck = self.chain
        if ck is not None and inckey is None:
            inckey, amt = ck, 1
        if self.dry:
            if inckey:
                self.cnt[inckey] += amt
                if event is not None:
                    self.rec[event] = self.cnt[inckey]
            return None
        inst = fn()
        if ck is not None:
            prev = self.cnt[ck]
            if event is None and inckey == ck:
                pass
            if prev > 0:
                inst._wait_ge(self.sems[ck], prev)
        if inckey:
            inst.then_inc(self.sems[inckey], amt)
            self.cnt[inckey] += amt
            if event is not None:
                assert self.rec.get(event) == self.cnt[inckey], (
                    event, self.rec.get(event), self.cnt[inckey])
        return inst

    def ev(self, name):
        return self.rec[name]

    def wait_ev(self, eng, key, event):
        if self.dry:
            return
        v = self.rec[event]
        if v > 0:
            eng.wait_ge(self.sems[key], v)

    def wait_cur(self, eng, key):
        if self.dry:
            return
        v = self.cnt[key]
        if v > 0:
            eng.wait_ge(self.sems[key], v)

    def wait_val(self, eng, key, v):
        if self.dry:
            return
        if v > 0:
            eng.wait_ge(self.sems[key], v)


def build_nc(stage="full"):
    nc = bass.Bass("TRN2", target_bir_lowering=False)

    xT_d = nc.dram_tensor("xT", [2048, TT], F32, kind="ExternalInput")
    wq_d = nc.dram_tensor("wq", [NH, 128, CH * 128], F32, kind="ExternalInput")
    wk_d = nc.dram_tensor("wk", [128, CH, 512], F32, kind="ExternalInput")
    wv_d = nc.dram_tensor("wv", [128, CH, 512], F32, kind="ExternalInput")
    wo_d = nc.dram_tensor("wo", [CH, 128, CH * 128], F32, kind="ExternalInput")
    wg_d = nc.dram_tensor("wg", [NE, 32, 128, CH * 128], F32,
                          kind="ExternalInput")
    wd_d = nc.dram_tensor("wd", [NE, 16, 128, CH * 128], F32,
                          kind="ExternalInput")
    lam_d = nc.dram_tensor("lam", [128, NH], F32, kind="ExternalInput")
    cos_d = nc.dram_tensor("cosT", [128, TT], F32, kind="ExternalInput")
    sin_d = nc.dram_tensor("sinT", [128, TT], F32, kind="ExternalInput")
    maskE_d = nc.dram_tensor("maskE", [5, 128, 256], BF16,
                             kind="ExternalInput")
    maskM_d = nc.dram_tensor("maskM", [3, 128, 384], BF16,
                             kind="ExternalInput")
    id_d = nc.dram_tensor("identT", [128, 128], BF16, kind="ExternalInput")
    yT_d = nc.dram_tensor("yT", [2048, TO], F32, kind="ExternalOutput")
    wdc_d = nc.dram_tensor("wdc", [NE, 16, 128, 2048], BF16, kind="Internal")

    ctx = ExitStack()
    with ctx:
        sems = {k: ctx.enter_context(nc.semaphore("sem" + k)) for k in SEMKEYS}

        def sbuf(name, shape, dt, side="left"):
            return ctx.enter_context(nc.sbuf_tensor(name, shape, dt, side=side))

        x_own = sbuf("x_own", [128, CH, TO], F32)
        A_sb = sbuf("A_sb", [128, TT], BF16)
        B_sb = sbuf("B_sb", [128, TT], BF16)
        lam_sb = sbuf("lam_sb", [128, NH], F32)
        ident = sbuf("ident_sb", [128, 128], BF16)
        cosq = sbuf("cosq", [128, TO], BF16)
        sinq = sbuf("sinq", [128, TO], BF16)
        onesK = sbuf("onesK", [128, 1], BF16)
        ones1 = sbuf("ones1", [1, 128], BF16)


        def run(E):
            real = not E.dry

            CHAINKEY = {"vector": "V", "scalar": "S"}
            ENG_ORDER = ("sync", "gpsimd", "tensor", "scalar", "vector")
            pending = {n: [] for n in ENG_ORDER}

            def block(*bodies):
                for name, fn in bodies:
                    pending[name].append(fn)

            def flush_program():
                def run_engine(name, eng):
                    E.chain = CHAINKEY.get(name)
                    for fn in pending[name]:
                        fn(eng)
                    E.chain = None
                if E.dry:
                    for name in ENG_ORDER:
                        run_engine(name, None)
                else:
                    with nc.Block() as blk:
                        for name in ENG_ORDER:
                            getattr(blk, name)(
                                lambda eng, name=name: run_engine(name, eng))

            rs = ExitStack()

            def ralloc(name, shape, dt):
                if real:
                    return rs.enter_context(
                        nc.sbuf_tensor(name, shape, dt, side="right"))
                return None

            def lalloc(stack, name, shape, dt):
                if real:
                    return stack.enter_context(
                        nc.sbuf_tensor(name, shape, dt, side="left"))
                return None

            def palloc(stack, name, shape, dt=F32):
                if real:
                    return stack.enter_context(
                        nc.psum_tensor(name, shape, dt))
                return None

            x_halo_ctx = ExitStack()
            h_ctx = ExitStack()
            x_halo = None
            h_buf = None
            tabf_ctx = ExitStack()
            kvtab_ctx = ExitStack()
            cosk = sink = None
            if real:
                h_buf = h_ctx.enter_context(
                    nc.sbuf_tensor("h_buf", [128, CH, TT], BF16, side="right"))
                cosk = kvtab_ctx.enter_context(
                    nc.sbuf_tensor("cosk", [128, TT], BF16, side="right"))
                sink = kvtab_ctx.enter_context(
                    nc.sbuf_tensor("sink", [128, TT], BF16, side="right"))
                x_halo = x_halo_ctx.enter_context(
                    nc.sbuf_tensor("x_halo", [128, CH, TH], F32, side="right"))
            cosf = None
            sinf = None
            if real:
                cosf = tabf_ctx.enter_context(
                    nc.sbuf_tensor("cosf", [128, TT], F32, side="right"))
                sinf = tabf_ctx.enter_context(
                    nc.sbuf_tensor("sinf", [128, TT], F32, side="right"))

            def p0_sync(sync):
                for gr in range(4):
                    r0, r1 = gr * 512, gr * 512 + 512
                    E.op(lambda r0=r0, r1=r1, gr=gr: sync.dma_start(
                        out=x_halo[:, gr * 4:gr * 4 + 4, :],
                        in_=xT_d[r0:r1, 0:TH].rearrange(
                            "(k p) t -> p k t", p=128)),
                        f"W{gr}", 16, event=f"ld_xh_{gr}")
                    E.op(lambda r0=r0, r1=r1, gr=gr: sync.dma_start(
                        out=x_own[:, gr * 4:gr * 4 + 4, :],
                        in_=xT_d[r0:r1, TH:TT].rearrange(
                            "(k p) t -> p k t", p=128)),
                        f"W{gr + 4}", 16, event=f"ld_xo_{gr}")
                E.op(lambda: sync.dma_start(out=cosf[:], in_=cos_d[:, :]),
                     "X", 16, event="ld_cos")
                E.op(lambda: sync.dma_start(out=sinf[:], in_=sin_d[:, :]),
                     "X", 16, event="ld_sin")
                E.op(lambda: sync.dma_start(out=lam_sb[:], in_=lam_d[:, :]),
                     "X", 16, event="ld_lam")
                E.op(lambda: sync.dma_start(out=ident[:], in_=id_d[:, :]),
                     "X", 16, event="ld_id")

            def p0_vector(v):
                E.op(lambda: v.memset(onesK[:], 1.0))
                E.op(lambda: v.memset(ones1[:], 1.0))
                E.wait_ev(v, "X", "ld_id")
                E.op(lambda: v.tensor_copy(cosk[:], cosf[:]))
                E.op(lambda: v.tensor_copy(sink[:], sinf[:]))
                E.op(lambda: v.tensor_copy(cosq[:], cosf[:, TH:TT]))
                E.op(lambda: v.tensor_copy(sinq[:], sinf[:, TH:TT]), "V", 1,
                     event="tab_done")

            block(("sync", p0_sync), ("vector", p0_vector))
            if real:
                tabf_ctx.close()

            def emit_ln(tag, regions, h_out, Ttot, pre_wait, packed=False,
                        x_wait=None):
                ln_ctx = ExitStack()
                xb_t = None
                if real:
                    xb_t = ln_ctx.enter_context(nc.sbuf_tensor(
                        tag + "_xb", [128, 2, Ttot], BF16, side="right"))
                    xq_t = ln_ctx.enter_context(nc.sbuf_tensor(
                        tag + "_xq", [128, 2, Ttot], BF16, side="right"))
                    m_row = ln_ctx.enter_context(nc.sbuf_tensor(
                        tag + "_m", [1, Ttot], F32, side="right"))
                    w_row = ln_ctx.enter_context(nc.sbuf_tensor(
                        tag + "_w", [1, Ttot], F32, side="right"))
                    a16 = ln_ctx.enter_context(nc.sbuf_tensor(
                        tag + "_a", [1, Ttot], BF16, side="right"))
                    b16 = ln_ctx.enter_context(nc.sbuf_tensor(
                        tag + "_b", [1, Ttot], BF16, side="right"))
                else:
                    xq_t = m_row = w_row = a16 = b16 = None
                st_ctx = ExitStack()
                sum_ps = palloc(st_ctx, tag + "_sum", [1, Ttot])
                sq_ps = palloc(st_ctx, tag + "_sq", [1, Ttot])

                def v_body(v):
                    pre_wait(v)
                    for k in range(CH):
                        if k >= 2:
                            E.wait_ev(v, "T", f"{tag}_st_{k-2}")
                            E.wait_ev(v, "G", f"{tag}_gsq_{k-2}")
                        if x_wait is not None:
                            x_wait(v, k)
                        for ri, (xb, off, ncol) in enumerate(regions):
                            E.op(lambda xb=xb, k=k, off=off, ncol=ncol:
                                 v.tensor_copy(
                                     xb_t[:, k % 2, off:off + ncol],
                                     xb()[:, k, :]),
                                 "V" if ri == len(regions) - 1 else None, 1,
                                 event=(f"{tag}_cast_{k}"
                                        if ri == len(regions) - 1 else None))
                    E.wait_ev(v, "T", f"{tag}_st_{CH-1}")
                    E.op(lambda: v.tensor_scalar(
                        m_row[:], sum_ps[:], 1.0 / 2048.0, None, ALU.mult))
                    E.op(lambda: v.scalar_tensor_tensor(
                        w_row[:], m_row[:], -1.0, m_row[:],
                        ALU.mult, ALU.mult))
                    E.op(lambda: v.scalar_tensor_tensor(
                        w_row[:], sq_ps[:], 1.0 / 2048.0, w_row[:],
                        ALU.mult, ALU.add))
                    E.op(lambda: v.tensor_scalar(
                        w_row[:], w_row[:], 1e-5, None, ALU.add),
                        "V", 1, event=f"{tag}_var")
                    E.wait_ev(v, "S", f"{tag}_rstd")
                    E.op(lambda: v.tensor_copy(a16[:], w_row[:]))
                    E.op(lambda: v.scalar_tensor_tensor(
                        b16[:], m_row[:], -1.0, w_row[:], ALU.mult, ALU.mult),
                        "V", 1, event=f"{tag}_ab")

                def gsq_body(g):
                    pre_wait(g)
                    for k in range(CH):
                        E.wait_ev(g, "V", f"{tag}_cast_{k}")
                        if k >= 2:
                            E.wait_ev(g, "T", f"{tag}_st_{k-2}")
                        for ri, (xb, off, ncol) in enumerate(regions):
                            E.op(lambda k=k, off=off, ncol=ncol:
                                 g.tensor_tensor(
                                     xq_t[:, k % 2, off:off + ncol],
                                     xb_t[:, k % 2, off:off + ncol],
                                     xb_t[:, k % 2, off:off + ncol],
                                     ALU.mult),
                                 "G" if ri == len(regions) - 1 else None, 1,
                                 event=(f"{tag}_gsq_{k}"
                                        if ri == len(regions) - 1 else None))

                def s_body(s):
                    pre_wait(s)
                    E.wait_ev(s, "V", f"{tag}_var")
                    E.op(lambda: s.activation(w_row[:], w_row[:], AF.Ln))
                    E.op(lambda: s.activation(w_row[:], w_row[:], AF.Exp,
                                              scale=-0.5),
                         "S", 1, event=f"{tag}_rstd")

                def t_body(t):
                    segs = []
                    for (xb, off, ncol) in regions:
                        for c0 in range(0, ncol, 512):
                            segs.append((off + c0, min(512, ncol - c0)))
                    for k in range(CH):
                        E.wait_ev(t, "V", f"{tag}_cast_{k}")
                        E.wait_ev(t, "G", f"{tag}_gsq_{k}")
                        for si, (s0, sn) in enumerate(segs):
                            E.op(lambda k=k, s0=s0, sn=sn: t.matmul(
                                sum_ps[0:1, s0:s0 + sn], onesK[:],
                                xb_t[:, k % 2, s0:s0 + sn],
                                start=(k == 0), stop=(k == CH - 1)))
                            last = si == len(segs) - 1
                            E.op(lambda k=k, s0=s0, sn=sn: t.matmul(
                                sq_ps[0:1, s0:s0 + sn], onesK[:],
                                xq_t[:, k % 2, s0:s0 + sn],
                                start=(k == 0), stop=(k == CH - 1)),
                                "T" if last else None, 1,
                                event=(f"{tag}_st_{k}" if last else None))

                block(("gpsimd", gsq_body),
                      ("tensor", t_body), ("scalar", s_body),
                      ("vector", v_body))
                if real:
                    st_ctx.close()
                ab_ctx = ExitStack()
                A_ps = palloc(ab_ctx, tag + "_A", [128, Ttot])
                B_ps = palloc(ab_ctx, tag + "_B", [128, Ttot])

                def t2_body(t):
                    E.wait_ev(t, "V", f"{tag}_ab")
                    nseg = (Ttot + 511) // 512
                    for ci in range(nseg):
                        c0, c1 = ci * 512, min(ci * 512 + 512, Ttot)
                        E.op(lambda c0=c0, c1=c1: t.matmul(
                            A_ps[:, c0:c1], ones1[:], a16[:, c0:c1],
                            start=True, stop=True))
                        E.op(lambda c0=c0, c1=c1: t.matmul(
                            B_ps[:, c0:c1], ones1[:], b16[:, c0:c1],
                            start=True, stop=True),
                            "T" if ci == nseg - 1 else None, 1,
                            event=(f"{tag}_bc" if ci == nseg - 1 else None))

                W4 = Ttot // 4

                def pk_dst(k, off, ncol):
                    return h_out()[:, k, :].rearrange(
                        "p (e q) -> p e q", q=W4)[:, :,
                                                  off // 4:off // 4 + ncol // 4]

                def pk_src(ap):
                    return ap.rearrange("p (u e) -> p e u", e=4)

                def v2_body(v):
                    E.wait_ev(v, "T", f"{tag}_bc")
                    E.op(lambda: v.tensor_copy(A_sb[:, 0:Ttot],
                                               A_ps[:, 0:Ttot]))
                    E.op(lambda: v.tensor_copy(B_sb[:, 0:Ttot],
                                               B_ps[:, 0:Ttot]),
                         "V", 1, event=f"{tag}_absb")
                    for k in range(CH):
                        for ri, (xb, off, ncol) in enumerate(regions):
                            lastr = ri == len(regions) - 1
                            ev = f"{tag}_h_{k}" if lastr else None
                            if packed:
                                E.op(lambda xb=xb, k=k, off=off, ncol=ncol:
                                     v.tensor_tensor(
                                         pk_dst(k, off, ncol),
                                         pk_src(xb()[:, k, :]),
                                         pk_src(A_sb[:, off:off + ncol]),
                                         ALU.mult))
                                E.op(lambda k=k, off=off, ncol=ncol:
                                     v.tensor_tensor(
                                         pk_dst(k, off, ncol),
                                         pk_dst(k, off, ncol),
                                         pk_src(B_sb[:, off:off + ncol]),
                                         ALU.add),
                                     "V" if lastr else None, 1, event=ev)
                                continue
                            E.op(lambda xb=xb, k=k, off=off, ncol=ncol:
                                 v.tensor_tensor(
                                     h_out()[:, k, off:off + ncol],
                                     xb()[:, k, :],
                                     A_sb[:, off:off + ncol], ALU.mult))
                            E.op(lambda k=k, off=off, ncol=ncol:
                                 v.tensor_tensor(
                                     h_out()[:, k, off:off + ncol],
                                     h_out()[:, k, off:off + ncol],
                                     B_sb[:, off:off + ncol], ALU.add),
                                 "V" if lastr else None, 1, event=ev)
                    if E.dry:
                        E.rec[f"{tag}_done"] = E.rec[f"{tag}_h_{CH-1}"]

                block(("tensor", t2_body), ("vector", v2_body))
                if real:
                    ab_ctx.close()
                    ln_ctx.close()

            def emit_ffn(tag, h_get, Ttot, iw, res_splits, write_cache,
                         read_cache, wring_get, h_tag, first_waits=None):
                ffn_ctx = ExitStack()
                if real:
                    silu_t = ffn_ctx.enter_context(nc.sbuf_tensor(
                        tag + "_silu", [128, 2, iw], BF16, side="right"))
                    inter = ffn_ctx.enter_context(nc.sbuf_tensor(
                        tag + "_int", [128, CH, iw], BF16, side="right"))
                else:
                    silu_t = inter = None
                pg = palloc(ffn_ctx, tag + "_pg", [128, 2, 512])
                pu = palloc(ffn_ctx, tag + "_pu", [128, 2, 512])
                pd = palloc(ffn_ctx, tag + "_pd", [128, 2, 512])

                def is_down(gi):
                    return gi % 48 >= 32

                def g_body(g):
                    gi = 0
                    for e in range(NE):
                        for i in range(32):
                            m, is_up = i // 2, i % 2
                            col0 = (2048 if is_up else 0) + m * 128
                            if gi == 0 and first_waits is not None:
                                first_waits(g)
                            if gi >= NSLOT:
                                E.wait_ev(g, "T", f"{tag}_ev_{gi-NSLOT}")
                                if write_cache and is_down(gi - NSLOT):
                                    E.wait_ev(g, f"C{gi % NSLOT}",
                                              f"{tag}_c_{gi-NSLOT}")
                            sl = gi % NSLOT
                            E.op(lambda e=e, i=i, sl=sl:
                                 g.dma_start(
                                     out=wring_get()[:, sl, :],
                                     in_=wg_d[e, i]),
                                 f"W{sl}", 16, event=f"{tag}_w_{gi}")
                            gi += 1
                        for m2 in range(16):
                            if gi >= NSLOT:
                                E.wait_ev(g, "T", f"{tag}_ev_{gi-NSLOT}")
                                if write_cache and is_down(gi - NSLOT):
                                    E.wait_ev(g, f"C{gi % NSLOT}",
                                              f"{tag}_c_{gi-NSLOT}")
                            sl = gi % NSLOT
                            if read_cache:
                                E.op(lambda e=e, m2=m2, sl=sl: g.dma_start(
                                    out=wring_get()[:, sl, :],
                                    in_=wdc_d[e, m2]),
                                    f"W{sl}", 16, event=f"{tag}_w_{gi}")
                            else:
                                E.op(lambda e=e, m2=m2, sl=sl: g.dma_start(
                                    out=wring_get()[:, sl, :],
                                    in_=wd_d[e, m2]),
                                    f"W{sl}", 16, event=f"{tag}_w_{gi}")
                            gi += 1

                def sync_body(sync):
                    if not write_cache:
                        return
                    gi = 0
                    for e in range(NE):
                        gi += 32
                        for m2 in range(16):
                            sl = gi % NSLOT
                            E.wait_ev(sync, f"W{sl}", f"{tag}_w_{gi}")
                            E.op(lambda e=e, m2=m2, sl=sl: sync.dma_start(
                                out=wdc_d[e, m2], in_=wring_get()[:, sl, :]),
                                f"C{sl}", 16, event=f"{tag}_c_{gi}")
                            gi += 1

                def t_body(t):
                    gi = 0
                    for e in range(NE):
                        for i in range(32):
                            m, is_up = i // 2, i % 2
                            ps = pu if is_up else pg
                            E.wait_ev(t, f"W{gi % NSLOT}", f"{tag}_w_{gi}")
                            if is_up:
                                if e * 16 + m >= 2:
                                    E.wait_ev(t, "V",
                                              f"{tag}_int_{e*16+m-2}")
                            else:
                                if e * 16 + m >= 2:
                                    E.wait_ev(t, "S",
                                              f"{tag}_silu_{e*16+m-2}")
                            sl = gi % NSLOT
                            for k in range(CH):
                                if gi == 0:
                                    E.wait_ev(t, "V", f"{h_tag}_h_{k}")
                                E.op(lambda ps=ps, m=m, k=k, sl=sl, e=e:
                                     t.matmul(
                                         ps[:, m % 2, 0:iw],
                                         wring_get()[:, sl,
                                                     k * 128:(k + 1) * 128],
                                         h_get()[:, k, e * iw:(e + 1) * iw],
                                         start=(k == 0), stop=(k == CH - 1)),
                                     "T" if k == CH - 1 else None, 1,
                                     event=(f"{tag}_ev_{gi}"
                                            if k == CH - 1 else None))
                            gi += 1
                        for m2 in range(16):
                            E.wait_ev(t, f"W{gi % NSLOT}", f"{tag}_w_{gi}")
                            if m2 == 0:
                                E.wait_ev(t, "V", f"{tag}_int_{e*16+15}")
                            if e * 16 + m2 >= 2:
                                E.wait_ev(t, "V", f"{tag}_res_{e*16+m2-2}")
                            sl = gi % NSLOT
                            for k in range(CH):
                                E.op(lambda m2=m2, k=k, sl=sl:
                                     t.matmul(
                                         pd[:, m2 % 2, 0:iw],
                                         wring_get()[:, sl,
                                                     k * 128:(k + 1) * 128],
                                         inter[:, k, :],
                                         start=(k == 0), stop=(k == CH - 1)),
                                     "T" if k == CH - 1 else None, 1,
                                     event=(f"{tag}_ev_{gi}"
                                            if k == CH - 1 else None))
                            gi += 1

                def s_body(s):
                    for e in range(NE):
                        for m in range(16):
                            gi = e * 48 + 2 * m
                            E.wait_ev(s, "T", f"{tag}_ev_{gi}")
                            if e * 16 + m >= 2:
                                E.wait_ev(s, "V", f"{tag}_int_{e*16+m-2}")
                            E.op(lambda m=m: s.activation(
                                silu_t[:, m % 2, :], pg[:, m % 2, 0:iw],
                                AF.Silu),
                                "S", 1, event=f"{tag}_silu_{e*16+m}")

                def v_body(v):
                    for e in range(NE):
                        for m in range(16):
                            gi = e * 48 + 2 * m + 1
                            E.wait_ev(v, "T", f"{tag}_ev_{gi}")
                            E.wait_ev(v, "S", f"{tag}_silu_{e*16+m}")
                            E.op(lambda m=m: v.tensor_tensor(
                                inter[:, m, :], silu_t[:, m % 2, :],
                                pu[:, m % 2, 0:iw], ALU.mult),
                                "V", 1, event=f"{tag}_int_{e*16+m}")
                        for m2 in range(16):
                            gi = e * 48 + 32 + m2
                            E.wait_ev(v, "T", f"{tag}_ev_{gi}")
                            for ri, (xg, pc0, ncol) in enumerate(res_splits):
                                last = ri == len(res_splits) - 1
                                E.op(lambda xg=xg, m2=m2, pc0=pc0, ncol=ncol,
                                     e=e: v.scalar_tensor_tensor(
                                         xg()[:, m2, e::4],
                                         pd[:, m2 % 2, pc0:pc0 + ncol], 0.5,
                                         xg()[:, m2, e::4],
                                         ALU.mult, ALU.add),
                                     "V" if last else None, 1,
                                     event=(f"{tag}_res_{e*16+m2}"
                                            if last else None))

                block(("gpsimd", g_body), ("sync", sync_body),
                      ("tensor", t_body), ("scalar", s_body),
                      ("vector", v_body))
                if real:
                    ffn_ctx.close()

            def pre_ln1(eng):
                E.wait_ev(eng, "V", "tab_done")

            def ln1_x_wait(eng, k):
                E.wait_ev(eng, f"W{k // 4}", f"ld_xh_{k // 4}")
                E.wait_ev(eng, f"W{k // 4 + 4}", f"ld_xo_{k // 4}")

            emit_ln("ln1", [(lambda: x_halo, 0, TH), (lambda: x_own, TH, TO)],
                    lambda: h_buf, TT, pre_ln1, packed=True,
                    x_wait=ln1_x_wait)

            ffn1_ring_ctx = ExitStack()
            wring1 = None
            if real:
                wring1 = ffn1_ring_ctx.enter_context(nc.sbuf_tensor(
                    "wring1", [128, NSLOT, CH * 128], BF16, side="right"))
            emit_ffn("ffn1", lambda: h_buf, TT, 384,
                     [(lambda: x_halo, 0, 128), (lambda: x_own, 128, 256)],
                     write_cache=False, read_cache=False,
                     wring_get=lambda: wring1, h_tag="ln1",
                     first_waits=lambda g: E.wait_ev(g, "T", "ln1_bc"))
            if real:
                ffn1_ring_ctx.close()

            def pre_ln2(eng):
                E.wait_ev(eng, "V", f"ffn1_res_{NE*16-1}")
                E.wait_ev(eng, "T", f"ffn1_ev_{NE*48-1}")

            emit_ln("ln2", [(lambda: x_halo, 0, TH), (lambda: x_own, TH, TO)],
                    lambda: h_buf, TT, pre_ln2)

            if stage == "ffn1":
                def out_sync(sync):
                    E.wait_ev(sync, "V", "ln2_done")
                    E.op(lambda: sync.dma_start(
                        out=yT_d[:, :].rearrange("(k p) t -> p k t", p=128),
                        in_=x_own[:, :, :]),
                        "O", 16)
                    E.wait_val(sync, "O", 16)
                block(("sync", out_sync))
                flush_program()
                return

            attn_left_ctx = ExitStack()
            ob = kb = vbuf = None
            if real:
                kb = attn_left_ctx.enter_context(nc.sbuf_tensor(
                    "kb", [128, NKV, TT], BF16, side="left"))
                vbuf = attn_left_ctx.enter_context(nc.sbuf_tensor(
                    "vbuf", [128, 12, 512], BF16, side="left"))
            kv_ctx2 = ExitStack()
            wkvb = tbk = None
            if real:
                wkvb = kv_ctx2.enter_context(nc.sbuf_tensor(
                    "wkvb", [128, CH, 512], BF16, side="right"))
                tbk = kv_ctx2.enter_context(nc.sbuf_tensor(
                    "tbk", [128, TT], BF16, side="right"))
            pskv_ctx = ExitStack()
            ps_k = palloc(pskv_ctx, "ps_k", [128, 2, TT])

            def kv_g(g):
                E.wait_ev(g, "V", "ln2_ab")
                E.wait_ev(g, "T", "ln2_bc")
                E.op(lambda: g.dma_start(
                    out=wkvb[:, :, :], in_=wk_d[:, :, :]),
                    "W", 16, event="w_wk")
                E.wait_ev(g, "T", "kmm_3")
                E.op(lambda: g.dma_start(
                    out=wkvb[:, :, :], in_=wv_d[:, :, :]),
                    "W", 16, event="w_wv")

            def kv_t(t):
                E.wait_ev(t, "W", "w_wk")
                for m in range(NKV):
                    if m >= 2:
                        E.wait_ev(t, "V", f"krope_{m-2}")
                    for ci in range(3):
                        c0, c1 = ci * 512, ci * 512 + 512
                        for k in range(CH):
                            if m == 0 and ci == 0:
                                E.wait_ev(t, "V", f"ln2_h_{k}")
                            last = (ci == 2 and k == CH - 1)
                            E.op(lambda m=m, k=k, c0=c0, c1=c1: t.matmul(
                                ps_k[:, m % 2, c0:c1],
                                wkvb[:, k, m * 128:(m + 1) * 128],
                                h_buf[:, k, c0:c1],
                                start=(k == 0), stop=(k == CH - 1)),
                                "T" if last else None, 1,
                                event=(f"kmm_{m}" if last else None))
                E.wait_ev(t, "W", "w_wv")
                for tv in range(12):
                    if tv >= 2:
                        E.wait_ev(t, "S", f"vevac_{tv-2}")
                    else:
                        E.wait_ev(t, "V", f"krope_{NKV - 2 + tv}")
                    for k in range(CH):
                        last = (k == CH - 1)
                        E.op(lambda tv=tv, k=k: t.matmul(
                            ps_k[:, tv % 2, 0:512],
                            h_buf[:, k, tv * 128:(tv + 1) * 128],
                            wkvb[:, k, :],
                            start=(k == 0), stop=(k == CH - 1)),
                            "T" if last else None, 1,
                            event=(f"vmm_{tv}" if last else None))

            def kv_s(s):
                for m in range(NKV):
                    E.wait_ev(s, "T", f"kmm_{m}")
                    E.op(lambda m=m: s.copy(kb[:, m, :],
                                            ps_k[:, m % 2, 0:TT]),
                         "S", 1, event=f"kevac_{m}")
                for tv in range(12):
                    E.wait_ev(s, "T", f"vmm_{tv}")
                    E.op(lambda tv=tv: s.copy(vbuf[:, tv, :],
                                              ps_k[:, tv % 2, 0:512]),
                         "S", 1, event=f"vevac_{tv}")

            def kv_v(v):
                for m in range(NKV):
                    E.wait_ev(v, "S", f"kevac_{m}")
                    E.op(lambda m=m: v.tensor_tensor(
                        tbk[0:64, :], ps_k[64:128, m % 2, 0:TT],
                        sink[0:64, :], ALU.mult))
                    E.op(lambda m=m: v.tensor_tensor(
                        tbk[64:128, :], ps_k[0:64, m % 2, 0:TT],
                        sink[64:128, :], ALU.mult))
                    E.op(lambda m=m: v.tensor_tensor(
                        kb[:, m, :], kb[:, m, :], cosk[:], ALU.mult))
                    E.op(lambda m=m: v.tensor_tensor(
                        kb[:, m, :], kb[:, m, :], tbk[:, :], ALU.add),
                        "V", 1, event=f"krope_{m}")

            block(("gpsimd", kv_g), ("tensor", kv_t), ("scalar", kv_s),
                  ("vector", kv_v))
            if real:
                pskv_ctx.close()
                kv_ctx2.close()
                x_halo_ctx.close()
                kvtab_ctx.close()

            at_ctx = ExitStack()
            wqr = qh = tbq = e1 = e2 = bA = pt_sb = mask_sb = None
            if real:
                ob = attn_left_ctx.enter_context(nc.sbuf_tensor(
                    "ob", [128, NH, TO], BF16, side="left"))
            if real:
                wqr = at_ctx.enter_context(nc.sbuf_tensor(
                    "wqr", [128, 2, CH * 128], BF16, side="right"))
                qh = at_ctx.enter_context(nc.sbuf_tensor(
                    "qh", [128, 3, TO], BF16, side="right"))
                tbq = at_ctx.enter_context(nc.sbuf_tensor(
                    "tbq", [128, TO], BF16, side="right"))
                e1 = at_ctx.enter_context(nc.sbuf_tensor(
                    "e1", [128, 2, WK], BF16, side="right"))
                e2 = at_ctx.enter_context(nc.sbuf_tensor(
                    "e2", [128, 2, WK], BF16, side="right"))
                bA = at_ctx.enter_context(nc.sbuf_tensor(
                    "bA", [128, WK], BF16, side="right"))
                pt_sb = at_ctx.enter_context(nc.sbuf_tensor(
                    "pt_sb", [128, WK], BF16, side="right"))
                mask_sb = at_ctx.enter_context(nc.sbuf_tensor(
                    "mask_sb", [128, 5, 256], BF16, side="right"))
                mid_sb = at_ctx.enter_context(nc.sbuf_tensor(
                    "mid_sb", [128, 3, 384], BF16, side="right"))
                rows = {}
                rows["sc2"] = at_ctx.enter_context(nc.sbuf_tensor(
                    "r_sc2", [128, 2, 2], F32, side="right"))
                rows["iv"] = at_ctx.enter_context(nc.sbuf_tensor(
                    "r_iv", [128, 2], F32, side="right"))
                rows["sd"] = at_ctx.enter_context(nc.sbuf_tensor(
                    "r_sd", [128, 1], F32, side="right"))
                rows["gam"] = at_ctx.enter_context(nc.sbuf_tensor(
                    "r_gam", [128, 2], F32, side="right"))
                rows["ivl"] = at_ctx.enter_context(nc.sbuf_tensor(
                    "r_ivl", [128, 2], F32, side="right"))
            else:
                rows = {rn: None for rn in ("sc2", "iv", "sd", "gam", "ivl")}
            ps_at_ctx = ExitStack()
            ps_s1 = palloc(ps_at_ctx, "ps_s1", [128, TO])
            ps_s2 = palloc(ps_at_ctx, "ps_s2", [128, TO])
            ps_pt = palloc(ps_at_ctx, "ps_pt", [128, TO], BF16)
            ps_oo = palloc(ps_at_ctx, "ps_oo", [128, 512])
            ps_q = palloc(ps_at_ctx, "ps_q", [128, TO])
            NIT = NH * NJ

            def at_sync(sync):
                E.wait_ev(sync, "T", "vmm_11")
                E.wait_ev(sync, "V", "krope_3")
                E.wait_ev(sync, "S", "vevac_11")
                E.op(lambda: sync.dma_start(
                    out=mask_sb[:],
                    in_=maskE_d[:, :, :].rearrange("j p c -> p j c")),
                    "X", 16)
                E.op(lambda: sync.dma_start(
                    out=mid_sb[:],
                    in_=maskM_d[:, :, :].rearrange("j p c -> p j c")),
                    "X", 16, event="ld_mask")

            def at_g(g):
                def emit_qdma(h):
                    if h == 0:
                        E.wait_ev(g, "T", "vmm_11")
                        E.wait_ev(g, "V", "krope_3")
                        E.wait_ev(g, "S", "vevac_11")
                    if h >= 2:
                        E.wait_ev(g, "T", f"qmm_{h-2}")
                    E.op(lambda h=h: g.dma_start(
                        out=wqr[:, h % 2, :], in_=wq_d[h]),
                        f"Q{h % 2}", 16, event=f"wqdma_{h}")

                emit_qdma(0)
                emit_qdma(1)
                for h in range(2, NH):
                    emit_qdma(h)

            def at_t(t):
                def q_mms(h):
                    E.wait_ev(t, f"Q{h % 2}", f"wqdma_{h}")
                    if h >= 1:
                        E.wait_ev(t, "V", f"qrope_{h-1}")
                    else:
                        E.wait_ev(t, "S", "vevac_11")
                        E.wait_ev(t, "V", "krope_3")
                        E.wait_ev(t, "X", "ld_id")
                    for ci in range(2):
                        c0, c1 = ci * 512, ci * 512 + 512
                        for k in range(CH):
                            last = (ci == 1 and k == CH - 1)
                            E.op(lambda h=h, k=k, c0=c0, c1=c1: t.matmul(
                                ps_q[:, c0:c1],
                                wqr[:, h % 2, k * 128:(k + 1) * 128],
                                h_buf[:, k, TH + c0:TH + c1],
                                start=(k == 0), stop=(k == CH - 1)),
                                "T" if last else None, 1,
                                event=(f"qmm_{h}" if last else None))

                def emit_scores(i):
                    h, j = i // NJ, i % NJ
                    kv = h // 4
                    c0 = j * 128
                    if j == 0:
                        E.wait_ev(t, "V", f"qrope_{h}")
                    for half in range(2):
                        if i >= 1:
                            E.wait_ev(t, "S", f"exp{half+1}_{i-1}")
                        d0 = half * 64
                        ps = ps_s1 if half == 0 else ps_s2
                        for si, (a0, a1) in enumerate(
                                [(0, 128), (128, 512), (512, 640)]):
                            do_mask = (si != 1) or (j <= 2)
                            E.op(lambda h=h, d0=d0, j=j, kv=kv, c0=c0,
                                 a0=a0, a1=a1, ps=ps, dm=do_mask: t.matmul(
                                     ps[:, a0:a1],
                                     qh[d0:d0 + 64, h % 3,
                                        j * 128:(j + 1) * 128],
                                     kb[d0:d0 + 64, kv,
                                        c0 + a0:c0 + a1],
                                     start=True, stop=not dm))
                            if do_mask:
                                last = (si == 2)
                                mj = min(j, 4)
                                if si == 0:
                                    msrc = lambda mj=mj: mask_sb[:, mj, 0:128]
                                elif si == 2:
                                    msrc = lambda mj=mj: mask_sb[:, mj,
                                                                 128:256]
                                else:
                                    msrc = lambda j=j: mid_sb[:, j, :]
                                E.op(lambda a0=a0, a1=a1, ps=ps, msrc=msrc:
                                     t.matmul(
                                         ps[:, a0:a1], ident[:],
                                         msrc(),
                                         start=False, stop=True),
                                     "T" if last else None, 1,
                                     event=(f"sc{half+1}_{i}"
                                            if last else None))

                q_mms(0)
                emit_scores(0)
                q_mms(1)
                emit_scores(1)
                for i in range(NIT):
                    h, j = i // NJ, i % NJ
                    kv = h // 4
                    E.wait_ev(t, "V", f"p_{i}")
                    if i >= 1:
                        E.wait_ev(t, "S", f"ptev_{i-1}")
                    for c in range(5):
                        E.op(lambda c=c, i=i: t.transpose(
                            ps_pt[:, c * 128:(c + 1) * 128],
                            bA[:, c * 128:(c + 1) * 128],
                            ident[:]),
                            "T" if c == 4 else None, 1,
                            event=(f"tr_{i}" if c == 4 else None))
                    if i >= 2 and (i - 2) % NJ == 0 and i // NJ + 2 < NH:
                        q_mms(i // NJ + 2)
                    if i + 2 < NIT:
                        emit_scores(i + 2)
                    E.wait_ev(t, "S", f"ptev_{i}")
                    if i >= 1:
                        E.wait_ev(t, "V", f"oev_{i-1}")
                    for c in range(5):
                        tv = j + c
                        E.op(lambda tv=tv, kv=kv, c=c: t.matmul(
                            ps_oo[:, 0:128],
                            vbuf[:, tv, kv * 128:(kv + 1) * 128],
                            pt_sb[:, c * 128:(c + 1) * 128],
                            start=(c == 0), stop=(c == 4)),
                            "T" if c == 4 else None, 1,
                            event=(f"pvd_{i}" if c == 4 else None))

            def at_s(s):
                def emit_qevac(h):
                    E.wait_ev(s, "T", f"qmm_{h}")
                    if h >= 3:
                        E.wait_ev(s, "T", f"sc2_{(h-3)*NJ + NJ - 1}")
                    E.op(lambda h=h: s.copy(qh[:, h % 3, :], ps_q[:, 0:TO]),
                         "S", 1, event=f"qevac_{h}")

                def emit_exp(i):
                    E.wait_ev(s, "T", f"sc1_{i}")
                    if i >= 2:
                        E.wait_ev(s, "V", f"p_{i-2}")
                    E.op(lambda i=i: s.activation(
                        e1[:, i % 2, :], ps_s1[:, 0:WK], AF.Exp, scale=0.125,
                        accum_out=rows["sc2"][:, i % 2, 0:1]),
                        "S", 1, event=f"exp1_{i}")
                    E.wait_ev(s, "T", f"sc2_{i}")
                    E.op(lambda i=i: s.activation(
                        e2[:, i % 2, :], ps_s2[:, 0:WK], AF.Exp, scale=0.125,
                        accum_out=rows["sc2"][:, i % 2, 1:2]),
                         "S", 1, event=f"exp2_{i}")

                emit_qevac(0)
                emit_qevac(1)
                for i in range(NIT):
                    if i % NJ == 4 and i // NJ + 2 < NH:
                        emit_qevac(i // NJ + 2)
                    if i >= 2:
                        _at_s_tail(s, i - 2)
                    emit_exp(i)
                _at_s_tail(s, NIT - 2)
                _at_s_tail(s, NIT - 1)

            def _at_s_tail(s, i):
                E.wait_ev(s, "T", f"tr_{i}")
                E.op(lambda i=i: s.copy(pt_sb[:], ps_pt[:, 0:WK]),
                     "S", 1, event=f"ptev_{i}")

            def at_v(v):
                def emit_rope(h):
                    E.wait_ev(v, "S", f"qevac_{h}")
                    E.op(lambda h=h: v.tensor_tensor(
                        tbq[0:64, :], ps_q[64:128, 0:TO],
                        sinq[0:64, :], ALU.mult))
                    E.op(lambda h=h: v.tensor_tensor(
                        tbq[64:128, :], ps_q[0:64, 0:TO],
                        sinq[64:128, :], ALU.mult))
                    E.op(lambda h=h: v.tensor_tensor(
                        qh[:, h % 3, :], qh[:, h % 3, :],
                        cosq[:], ALU.mult))
                    E.op(lambda h=h: v.tensor_tensor(
                        qh[:, h % 3, :], qh[:, h % 3, :], tbq[:, :],
                        ALU.add),
                        "V", 1, event=f"qrope_{h}")

                E.wait_ev(v, "X", "ld_mask")
                emit_rope(0)
                emit_rope(1)
                for i in range(NIT):
                    if i % NJ == 0 and 2 <= i // NJ + 1 < NH:
                        emit_rope(i // NJ + 1)
                    _at_v_iter(v, i // NJ, i % NJ)
                _at_v_oev(v, NIT - 2)
                _at_v_oev(v, NIT - 1)

            def _at_v_iter(v, h, j):
                i = h * NJ + j
                E.wait_ev(v, "S", f"exp2_{i}")
                E.op(lambda i=i: v.reciprocal(rows["iv"][:],
                                              rows["sc2"][:, i % 2, :]))
                E.op(lambda h=h, i=i: v.tensor_tensor(
                    rows["ivl"][:, i % 2:i % 2 + 1], rows["iv"][:, 1:2],
                    lam_sb[:, h:h + 1], ALU.mult))
                E.op(lambda i=i: v.tensor_scalar(
                    e2[:, i % 2, :], e2[:, i % 2, :],
                    rows["ivl"][:, i % 2:i % 2 + 1], None, ALU.mult))
                E.op(lambda i=i: v.scalar_tensor_tensor(
                    e1[:, i % 2, :], e1[:, i % 2, :], rows["iv"][:, 0:1],
                    e2[:, i % 2, :], ALU.mult, ALU.subtract))
                E.op(lambda i=i: v.tensor_scalar(
                    e1[:, i % 2, :], e1[:, i % 2, :], 0.0, 0.0,
                    ALU.max, ALU.add, accum_out=rows["sd"][:]))
                E.op(lambda: v.tensor_scalar(
                    rows["sd"][:], rows["sd"][:], 1e-6, None, ALU.add))
                E.op(lambda i=i: v.reciprocal(rows["gam"][:, i % 2:i % 2 + 1],
                                              rows["sd"][:]))
                if i >= 1:
                    E.wait_ev(v, "T", f"tr_{i-1}")
                E.op(lambda i=i: v.tensor_scalar(
                    bA[:], e1[:, i % 2, :], rows["gam"][:, i % 2:i % 2 + 1],
                    None, ALU.mult),
                    "V", 1, event=f"p_{i}")
                if i >= 2:
                    _at_v_oev(v, i - 2)

            def _at_v_oev(v, i):
                h, j = i // NJ, i % NJ
                E.wait_ev(v, "T", f"pvd_{i}")
                E.op(lambda h=h, j=j: v.tensor_copy(
                    ob[:, h, j * 128:(j + 1) * 128], ps_oo[:, 0:128]),
                    "V", 1, event=f"oev_{i}")

            block(("sync", at_sync), ("gpsimd", at_g), ("tensor", at_t),
                  ("scalar", at_s), ("vector", at_v))
            if real:
                ps_at_ctx.close()

            ps_wo_ctx = ExitStack()
            ps_wo = palloc(ps_wo_ctx, "ps_wo", [128, 2, TO])

            def wo_g(g):
                for m in range(CH):
                    if m < 2:
                        E.wait_ev(g, "T", f"qmm_{NH-1}")
                    else:
                        E.wait_ev(g, "T", f"womm_{m-2}")
                    E.op(lambda m=m: g.dma_start(
                        out=wqr[:, m % 2, :], in_=wo_d[m]),
                        f"Q{m % 2}", 16, event=f"wodma_{m}")

            def wo_t(t):
                E.wait_ev(t, "V", f"oev_{NH*NJ-1}")
                for m in range(CH):
                    E.wait_ev(t, f"Q{m % 2}", f"wodma_{m}")
                    if m >= 2:
                        E.wait_ev(t, "V", f"wores_{m-2}")
                    for ci in range(2):
                        c0, c1 = ci * 512, ci * 512 + 512
                        for k in range(CH):
                            last = (ci == 1 and k == CH - 1)
                            E.op(lambda m=m, k=k, c0=c0, c1=c1: t.matmul(
                                ps_wo[:, m % 2, c0:c1],
                                wqr[:, m % 2, k * 128:(k + 1) * 128],
                                ob[:, k, c0:c1],
                                start=(k == 0), stop=(k == CH - 1)),
                                "T" if last else None, 1,
                                event=(f"womm_{m}" if last else None))

            def wo_v(v):
                for m in range(CH):
                    E.wait_ev(v, "T", f"womm_{m}")
                    E.op(lambda m=m: v.tensor_tensor(
                        x_own[:, m, :], x_own[:, m, :], ps_wo[:, m % 2, :],
                        ALU.add),
                        "V", 1, event=f"wores_{m}")

            block(("gpsimd", wo_g), ("tensor", wo_t), ("vector", wo_v))
            if real:
                ps_wo_ctx.close()
                at_ctx.close()
                h_ctx.close()
                attn_left_ctx.close()

            if stage == "attn":
                def out_sync2(sync):
                    E.wait_ev(sync, "V", f"wores_{CH-1}")
                    E.op(lambda: sync.dma_start(
                        out=yT_d[:, :].rearrange("(k p) t -> p k t", p=128),
                        in_=x_own[:, :, :]),
                        "O", 16)
                    E.wait_val(sync, "O", 16)
                block(("sync", out_sync2))
                flush_program()
                return

            ffn2_ctx = ExitStack()
            h3 = wring2 = None
            if real:
                h3 = ffn2_ctx.enter_context(nc.sbuf_tensor(
                    "h3", [128, CH, TO], BF16, side="right"))
                wring2 = ffn2_ctx.enter_context(nc.sbuf_tensor(
                    "wring2", [128, NSLOT, CH * 128], BF16, side="right"))

            def pre_ln3(eng):
                E.wait_ev(eng, "V", f"wores_{CH-1}")
                E.wait_ev(eng, "T", f"womm_{CH-1}")
                E.wait_ev(eng, "V", f"oev_{NH*NJ-1}")

            emit_ln("ln3", [(lambda: x_own, 0, TO)], lambda: h3, TO, pre_ln3,
                    packed=True)
            def ffn2_first(g):
                E.wait_ev(g, "T", f"womm_{CH-1}")

            emit_ffn("ffn2", lambda: h3, TO, 256,
                     [(lambda: x_own, 0, 256)],
                     write_cache=False, read_cache=False,
                     wring_get=lambda: wring2, h_tag="ln3",
                     first_waits=ffn2_first)
            if real:
                ffn2_ctx.close()

            def out_sync3(sync):
                E.wait_ev(sync, "V", f"ffn2_res_{NE*16-1}")
                E.op(lambda: sync.dma_start(
                    out=yT_d[:, :].rearrange("(k p) t -> p k t", p=128),
                    in_=x_own[:, :, :]),
                    "O", 16)
                E.wait_val(sync, "O", 16)

            block(("sync", out_sync3))
            flush_program()
            if real:
                rs.close()

        dryE = Em(dry=True)
        run(dryE)
        realE = Em(dry=False, sems=sems, events=dryE.rec)
        run(realE)
        for k in SEMKEYS:
            assert dryE.cnt[k] == realE.cnt[k], (k, dryE.cnt[k], realE.cnt[k])
    return nc



def _mk_tables(core):
    p0 = core * TO - TH
    t = (np.arange(TT, dtype=np.float64) + p0).astype(np.float32)
    inv = (1.0 / (10000.0 ** (np.arange(0, 128, 2, dtype=np.float32)
                              / 128.0))).astype(np.float32)
    ang = t[None, :] * inv[:, None]
    ang = np.concatenate([ang, ang], axis=0)
    cos = np.cos(ang).astype(np.float32)
    sin = np.sin(ang).astype(np.float32)
    sinS = sin.copy()
    sinS[0:64] = -sin[0:64]
    return np.ascontiguousarray(cos), np.ascontiguousarray(sinS)


def _mk_mask(core):
    j = np.arange(NJ)[:, None, None]
    r = np.arange(128)[None, :, None]
    k = np.arange(WK)[None, None, :]
    lq = TH + 128 * j + r
    lk = 128 * j + k
    gk = core * TO - TH + lk
    allowed = (lk >= lq - 511) & (lk <= lq) & (gk >= 0)
    m = np.where(allowed, 0.0, -1e9).astype(ml_dtypes.bfloat16)
    edge = np.zeros((5, 128, 256), dtype=ml_dtypes.bfloat16)
    for jj in range(5):
        jsrc = jj if jj < 4 else 4
        edge[jj, :, 0:128] = m[jsrc, :, 0:128]
        edge[jj, :, 128:256] = m[jsrc, :, 512:640]
    mid = np.ascontiguousarray(m[0:3, :, 128:512])
    return np.ascontiguousarray(edge), mid


def _pack_cols(w):
    nb = w.shape[1] // 128
    t = w.reshape(16, 128, nb, 128).transpose(2, 1, 0, 3)
    return np.ascontiguousarray(t.reshape(nb, 128, 16 * 128))


def kernel(L0_attn_x, L0_attn_Wq, L0_attn_Wk, L0_attn_Wv, L0_attn_Wo,
           L0_attn_lam, L0_ffn_gate_up_proj, L0_ffn_down_proj):
    x = np.ascontiguousarray(np.asarray(L0_attn_x, dtype=np.float32))
    wq = _pack_cols(np.asarray(L0_attn_Wq, dtype=np.float32))
    wk = np.ascontiguousarray(
        np.asarray(L0_attn_Wk, dtype=np.float32)
        .reshape(16, 128, 512).transpose(1, 0, 2))
    wv = np.ascontiguousarray(
        np.asarray(L0_attn_Wv, dtype=np.float32)
        .reshape(16, 128, 512).transpose(1, 0, 2))
    wo = _pack_cols(np.asarray(L0_attn_Wo, dtype=np.float32))
    lam = np.asarray(L0_attn_lam, dtype=np.float32)
    wg_raw = np.asarray(L0_ffn_gate_up_proj, dtype=np.float32)
    wd_raw = np.asarray(L0_ffn_down_proj, dtype=np.float32)
    gu_order = [16 * (i % 2) + i // 2 for i in range(32)]
    wg = np.stack([_pack_cols(wg_raw[e])[gu_order] for e in range(NE)])
    wd = np.stack([_pack_cols(wd_raw[e]) for e in range(NE)])

    xT = np.ascontiguousarray(x[0].T)
    lam_rep = np.ascontiguousarray(
        np.broadcast_to(lam[None, :], (128, NH)).astype(np.float32))
    idm = np.ascontiguousarray(np.eye(128, dtype=ml_dtypes.bfloat16))

    nc = build_nc(_STAGE)
    in_maps = []
    for c in range(NCORE):
        lo = c * TO - TH
        if lo < 0:
            xc = np.zeros((2048, TT), dtype=np.float32)
            xc[:, TH:] = xT[:, 0:TO]
        else:
            xc = np.ascontiguousarray(xT[:, lo:lo + TT])
        cos, sin = _mk_tables(c)
        edge, mid = _mk_mask(c)
        in_maps.append({
            "xT": xc, "wq": wq, "wk": wk, "wv": wv, "wo": wo,
            "wg": wg, "wd": wd, "lam": lam_rep, "cosT": cos, "sinT": sin,
            "maskE": edge, "maskM": mid, "identT": idm,
        })
    res = run_bass_kernel_spmd(nc, in_maps, core_ids=list(range(NCORE)))
    parts = [np.asarray(res.results[c]["yT"], dtype=np.float32).T
             for c in range(NCORE)]
    out = np.concatenate(parts, axis=0).reshape(1, 8192, 2048)
    return np.ascontiguousarray(out)
